# version 44
# baseline (speedup 1.0000x reference)
import math
import contextlib
import numpy as np
import ml_dtypes
import concourse.bass as bass
import concourse.mybir as mybir
from concourse.bass_utils import run_bass_kernel_spmd

F32 = mybir.dt.float32
F16 = mybir.dt.float16
BF16 = mybir.dt.bfloat16
I32 = mybir.dt.int32
AX = mybir.AxisListType.X
ALU = mybir.AluOpType
ACT = mybir.ActivationFunctionType

B, L, D, FF, H, DH = 4, 4096, 1024, 3072, 16, 64
TOK, TA = 2048, 4096
DC, FC = D // 128, FF // 128
NQC, NTC = TOK // 512, TA // 512
NPAIR = H // 2
QMAX = 32767.0
DEBUG = False


class Sched:

    def __init__(self):
        self.ops = []

    def op(self, ek, fn, reads=(), writes=(), dma=False):
        tgt = self._cap if self._cap is not None else self.ops
        tgt.append((ek, fn, tuple(reads), tuple(writes), dma))

    _cap = None

    def begin_capture(self):
        self._cap = []

    def end_capture(self):
        c = self._cap
        self._cap = None
        return c

    def barrier(self):
        self.ops.append(("BARRIER", None, (), (), False))

    def plan(self):
        NDMA = 16
        dma_slots = [f"Y{i}" for i in range(NDMA)]
        cnt = {k: 0 for k in list("TVSG") + dma_slots}
        know = {k: {} for k in list("TVSG") + ["Y"]}
        wtag, rtags = {}, {}
        barrier = {}
        seen_keys = set()
        planned = []
        ndma = 0
        for ek, fn, reads, writes, dma in self.ops:
            if ek == "BARRIER":
                barrier = dict(cnt)
                continue
            need = {}

            def add(dep):
                if dep is None:
                    return
                s, v = dep
                if s == ek and ek == "T":
                    return
                if know[ek].get(s, 0) >= v:
                    return
                if need.get(s, 0) < v:
                    need[s] = v

            for key in reads:
                add(wtag.get(key))
            for key in writes:
                if key not in seen_keys and barrier:
                    for s, v in barrier.items():
                        if v:
                            add((s, v))
                add(wtag.get(key))
                for dep in rtags.get(key, ()):
                    add(dep)
            if ek == "Y":
                assert dma
                slot = dma_slots[ndma % NDMA]
                cyc = ndma // NDMA
                ndma += 1
                if cyc > 0:
                    add((slot, 16 * cyc))
                waits = sorted(need.items())
                for s, v in waits:
                    know[ek][s] = v
                cnt[slot] += 16
                tag = (slot, cnt[slot])
            else:
                waits = sorted(need.items())
                for s, v in waits:
                    know[ek][s] = v
                cnt[ek] += 1
                tag = (ek, cnt[ek])
            for key in reads:
                rtags.setdefault(key, []).append(tag)
            for key in writes:
                seen_keys.add(key)
                wtag[key] = tag
                rtags[key] = []
            planned.append((ek, fn, waits, 16 if ek == "Y" else 1))
        self.final_dma = {s: cnt[s] for s in dma_slots}
        return planned


def rope_maps(roll):
    pos = (np.arange(TA, dtype=np.float64) + roll) % L
    freqs = np.exp(-math.log(10000.0)
                   * np.arange(0, DH, 2, dtype=np.float64) / DH)
    ang = pos[None, :] * freqs[:, None]
    c32, s32 = np.cos(ang), np.sin(ang)
    cmap = np.tile(c32, (4, 1))
    smap = np.concatenate([s32, -s32, s32, -s32], axis=0)
    return (np.ascontiguousarray(cmap).astype(np.float16),
            np.ascontiguousarray(smap).astype(np.float16))


def head_perm():
    p = []
    for hh in range(H):
        p.extend((hh * 64 + 2 * np.arange(32)).tolist())
        p.extend((hh * 64 + 2 * np.arange(32) + 1).tolist())
    return np.array(p)


def build():
    nc = bass.Bass("TRN2", target_bir_lowering=False)
    xi = nc.dram_tensor("xi", [D, TA], F32, kind="ExternalInput").ap()
    wq_h = nc.dram_tensor("wq", [NPAIR, D, 128], F32,
                          kind="ExternalInput").ap()
    wk_h = nc.dram_tensor("wk", [NPAIR, D, 128], F32,
                          kind="ExternalInput").ap()
    wv_h = nc.dram_tensor("wv", [NPAIR, D, 128], F32,
                          kind="ExternalInput").ap()
    wo_h = nc.dram_tensor("wo", [D, D], F32, kind="ExternalInput").ap()
    wupt = nc.dram_tensor("wupt", [FC, D, 128], F32, kind="ExternalInput").ap()
    wgt = nc.dram_tensor("wgt", [FC, D, 128], F32, kind="ExternalInput").ap()
    wdt = nc.dram_tensor("wdt", [DC, FF, 128], F32, kind="ExternalInput").ap()
    wup_oi = nc.dram_tensor("wup_oi", [FF, D], F32, kind="ExternalInput").ap()
    wg_oi = nc.dram_tensor("wg_oi", [FF, D], F32, kind="ExternalInput").ap()
    wd_oi = nc.dram_tensor("wd_oi", [D, FF], F32, kind="ExternalInput").ap()
    cmap_h = nc.dram_tensor("cmap", [128, TA], F16, kind="ExternalInput").ap()
    smap_h = nc.dram_tensor("smap", [128, TA], F16, kind="ExternalInput").ap()
    sel8_h = nc.dram_tensor("sel8", [32, DC, 128], F32,
                            kind="ExternalInput").ap()
    sel32_h = nc.dram_tensor("sel32", [32, 8, 128], F32,
                             kind="ExternalInput").ap()
    yo = nc.dram_tensor("yo", [D, TOK], F32, kind="ExternalOutput").ap()
    dbg = {}
    if DEBUG:
        for nm, shp, dt in (("dbg_h1", [128, DC, TA], BF16),
                            ("dbg_Qr", [128, TOK], BF16),
                            ("dbg_Kr", [128, TA], BF16),
                            ("dbg_Vt", [128, 32, 130], BF16),
                            ("dbg_O", [128, DC, TOK], BF16),
                            ("dbg_x2", [128, DC, TOK], F32),
                            ("dbg_h2q", [128, DC, TOK], BF16),
                            ("dbg_gate", [128, FC, TOK], F16),
                            ("dbg_wub", [128, DC, 128], BF16)):
            dbg[nm] = nc.dram_tensor(nm, shp, dt, kind="ExternalOutput").ap()
    upsp = nc.dram_tensor("upsp", [FC, 128, TOK], F16, kind="Internal").ap()
    x2sp = nc.dram_tensor("x2sp", [DC, 128, TOK], F32, kind="Internal").ap()

    es0 = contextlib.ExitStack()
    sc = Sched()
    op = sc.op
    with es0:
        def mk(stack):
            def sb(name, shape, dtype):
                return stack.enter_context(nc.sbuf_tensor(name, shape, dtype))
            return sb

        sb0 = mk(es0)
        s_ones1 = sb0("s_ones1", [128, 1], BF16)
        s_onesk = sb0("s_onesk", [1, 128], F32)

        s_m = sb0("s_m", [1, 512], F32)
        s_r = sb0("s_r", [1, 512], F32)
        s_v1 = sb0("s_v1", [1, 512], F32)
        s_msq = sb0("s_msq", [1, 512], F32)
        s_rec = sb0("s_rec", [1, 512], F32)
        s_sc1 = sb0("s_sc1", [1, 1], F32)
        s_sc2 = sb0("s_sc2", [1, 1], F32)
        s_red = sb0("s_red", [128, 1], F32)
        s_tmp1 = sb0("s_tmp1", [128, 512], F32)
        s_t32 = sb0("s_t32", [32, 32], F32)
        s_t32b = sb0("s_t32b", [32, 32], F32)
        s_m3 = sb0("s_m3", [32, 3], F32)
        s_red2 = sb0("s_red2", [128, 1], F32)
        s_zero1 = sb0("s_zero1", [128, 1], F32)
        esO = contextlib.ExitStack()
        s_O = mk(esO)("s_O", [128, DC, TOK], BF16)

        pA = es0.enter_context(nc.psum_tensor("pA", [128, 1024], F32))
        pB = es0.enter_context(nc.psum_tensor("pB", [128, 1024], F32))
        p4 = es0.enter_context(nc.psum_tensor("p4", [128, 512], F32))
        p5 = es0.enter_context(nc.psum_tensor("p5", [128, 512], F32))
        p6 = es0.enter_context(nc.psum_tensor("p6", [128, 512], F32))
        p7 = es0.enter_context(nc.psum_tensor("p7", [128, 512], F32))

        sems = {k: es0.enter_context(nc.semaphore(f"sem{k}"))
                for k in list("TVSG") + [f"Y{i}" for i in range(16)]}
        blk = es0.enter_context(nc.Block())

        op("V", lambda e: e.memset(s_ones1[:], 1.0), writes=["ones1"])
        op("V", lambda e: e.memset(s_zero1[:], 0.0), writes=["zero1"])
        op("V", lambda e: e.memset(s_onesk[:], 1.0), writes=["onesk"])


        es1 = contextlib.ExitStack()
        sb1 = mk(es1)
        s_cmap = sb1("s_cmap", [128, TA], F16)
        s_smap = sb1("s_smap", [128, TA], F16)
        s_wq = [sb1(f"s_wq{i}", [128, DC, 128], BF16) for i in range(2)]
        s_wk = [sb1(f"s_wk{i}", [128, DC, 128], BF16) for i in range(2)]
        s_wv = [sb1(f"s_wv{i}", [128, DC, 128], BF16) for i in range(2)]
        s_wst = [sb1(f"s_wst{i}", [128, DC, 128], F32) for i in range(2)]
        s_h1 = sb1("s_h1", [128, DC, TA], BF16)
        es1a = contextlib.ExitStack()
        sb1a = mk(es1a)
        s_xf = sb1a("s_xf", [128, DC, 512], F32)
        s_xb = sb1a("s_xb", [128, DC, 512], BF16)
        s_xsq = sb1a("s_xsq", [128, 512], BF16)
        s_stage = sb1a("s_stage", [128, D], F32)

        op("Y", lambda e: e.dma_start(out=s_cmap[:], in_=cmap_h),
           writes=["cmap"], dma=True)
        op("Y", lambda e: e.dma_start(out=s_smap[:], in_=smap_h),
           writes=["smap"], dma=True)

        def ln_block(xsrc, xap_f, xkeys, outap_f, outkey_f, mbank, mkey,
                     bbank1, bbank2, bkey1, bkey2):
            op("V", lambda e, xsrc=xsrc: e.tensor_copy(
                s_xb[:], xsrc), reads=xkeys, writes=["xb"])
            for c in range(DC):
                op("S", lambda e, c=c: e.square(s_xsq[:], s_xb[:, c, :]),
                   reads=["xb"], writes=["xsq"])
                op("T", lambda e, c=c, mbank=mbank: e.matmul(
                    mbank[0:1, 0:512], s_ones1[:], s_xb[:, c, :],
                    start=(c == 0), stop=(c == DC - 1), tile_position=(0, 0),
                    skip_group_check=True),
                   reads=["ones1", "xb"], writes=[mkey])
                op("T", lambda e, c=c, mbank=mbank: e.matmul(
                    mbank[32:33, 0:512], s_ones1[:], s_xsq[:],
                    start=(c == 0), stop=(c == DC - 1), tile_position=(0, 32),
                    skip_group_check=True),
                   reads=["ones1", "xsq"], writes=[mkey])
            op("V", lambda e, mbank=mbank: e.tensor_scalar(
                s_m[:], mbank[0:1, 0:512], 1.0 / D, None, ALU.mult),
               reads=[mkey], writes=["m"])
            op("V", lambda e, mbank=mbank: e.tensor_scalar(
                s_v1[:], mbank[32:33, 0:512], 1.0 / D, None, ALU.mult),
               reads=[mkey], writes=["v1"])
            op("V", lambda e: e.tensor_mul(s_msq[:], s_m[:], s_m[:]),
               reads=["m"], writes=["msq"])
            op("V", lambda e: e.tensor_sub(s_v1[:], s_v1[:], s_msq[:]),
               reads=["msq"], writes=["v1"])
            op("V", lambda e: e.tensor_scalar(
                s_v1[:], s_v1[:], 1e-5, None, ALU.add), writes=["v1"])
            op("V", lambda e: e.reciprocal(s_rec[:], s_v1[:]),
               reads=["v1"], writes=["rec"])
            op("S", lambda e: e.sqrt(s_r[:], s_rec[:]),
               reads=["rec"], writes=["r"])
            op("T", lambda e, bbank1=bbank1: e.matmul(
                bbank1[:, 0:512], s_onesk[:], s_m[:], start=True, stop=True),
               reads=["onesk", "m"], writes=[bkey1])
            op("T", lambda e, bbank2=bbank2: e.matmul(
                bbank2[:, 0:512], s_onesk[:], s_r[:], start=True, stop=True),
               reads=["onesk", "r"], writes=[bkey2])
            for c in range(DC):
                op("V", lambda e, c=c, bbank1=bbank1: e.tensor_sub(
                    s_tmp1[:], xap_f(c), bbank1[:, 0:512]),
                   reads=[xkeys[c], bkey1], writes=["tmp1"])
                op("V", lambda e, c=c, bbank2=bbank2: e.tensor_mul(
                    outap_f(c), s_tmp1[:], bbank2[:, 0:512]),
                   reads=["tmp1", bkey2], writes=[outkey_f(c)])

        for tc in range(NTC):
            src = xi[:, 512 * tc:512 * (tc + 1)].rearrange(
                "(c p) t -> p c t", p=128)
            op("Y", lambda e, src=src: e.dma_start(out=s_xf[:], in_=src),
               writes=["xf"], dma=True)
            ln_block(s_xf[:], lambda c: s_xf[:, c, :], ["xf"] * DC,
                     lambda c, tc=tc: s_h1[:, c, 512 * tc:512 * (tc + 1)],
                     lambda c, tc=tc: f"h1:{c}:{tc}",
                     p4, "p4", p5, p6, "p5", "p6")

        if DEBUG:
            op("Y", lambda e: e.dma_start(out=dbg["dbg_h1"], in_=s_h1[:]),
               reads=[f"h1:{c}:{t}" for c in range(DC) for t in range(NTC)],
               writes=["dbg1"], dma=True)
        es1a.close()
        sc.barrier()
        s_Qr = [sb1(f"s_Qr{i}", [128, TOK], BF16) for i in range(2)]
        s_Kr = [sb1(f"s_Kr{i}", [128, TA], BF16) for i in range(2)]
        s_Vt = [sb1(f"s_Vt{i}", [128, 32, 130], BF16) for i in range(2)]
        s_T1 = sb1("s_T1", [128, 512], F32)
        s_T2 = sb1("s_T2", [128, 512], F32)
        s_T2s = sb1("s_T2s", [128, 512], F32)
        s_pvS = [sb1(f"s_pvS{i}", [128, 512], F32) for i in range(2)]
        s_l2 = sb1("s_l2", [1, 1024], F32)
        s_l2r = sb1("s_l2r", [1, 1024], F32)
        s_PtA = [sb1(f"s_PtA{i}", [128, 512], BF16) for i in range(2)]
        s_PtB = [sb1(f"s_PtB{i}", [128, 512], BF16) for i in range(2)]

        for i in range(2):
            op("V", lambda e, i=i: e.memset(s_Vt[i][:, :, 64:65], 1.0),
               writes=[f"Vt{i}"])
            op("V", lambda e, i=i: e.memset(s_Vt[i][:, :, 129:130], 1.0),
               writes=[f"Vt{i}"])

        def p2_ops(p, bi):
            sc.begin_capture()
            for nm, hsrc, dstw in (("wq", wq_h, s_wq), ("wk", wk_h, s_wk),
                                   ("wv", wv_h, s_wv)):
                st_ = s_wst[0] if nm != "wv" else s_wst[1]
                stk = "wst0" if nm != "wv" else "wst1"
                srcap = hsrc[p].rearrange("(c p) o -> p c o", p=128)
                op("Y", lambda e, srcap=srcap, st_=st_: e.dma_start(
                    out=st_[:], in_=srcap), writes=[stk], dma=True)
                op("V", lambda e, dstw=dstw, st_=st_, bi=bi: e.tensor_copy(
                    dstw[bi][:], st_[:]),
                   reads=[stk], writes=[f"{nm}{bi}"])
            for qc in range(NQC):
                qpb, qbk = (p6, "p6") if qc % 2 == 0 else (p7, "p7")
                for c in range(DC):
                    op("T", lambda e, c=c, qc=qc, qpb=qpb, bi=bi: e.matmul(
                        qpb[:], s_wq[bi][:, c, :],
                        s_h1[:, c, 512 * qc:512 * (qc + 1)],
                        start=(c == 0), stop=(c == DC - 1),
                        skip_group_check=True),
                       reads=[f"wq{bi}", f"h1:{c}:{qc}"], writes=[qbk])
                qsl = slice(512 * qc, 512 * (qc + 1))
                op("V", lambda e, qpb=qpb, qsl=qsl: e.tensor_mul(
                    s_T1[:], qpb[:], s_cmap[:, qsl]),
                   reads=[qbk, "cmap"], writes=["T1"])
                op("V", lambda e, qpb=qpb, qsl=qsl: e.tensor_mul(
                    s_T2[:], qpb[:], s_smap[:, qsl]),
                   reads=[qbk, "smap"], writes=["T2"])
                for b4 in range(4):
                    o4 = slice(32 * b4, 32 * (b4 + 1))
                    x4 = slice(32 * (b4 ^ 1), 32 * ((b4 ^ 1) + 1))
                    op("V", lambda e, o4=o4, x4=x4: e.tensor_copy(
                        s_T2s[o4, :], s_T2[x4, :]),
                       reads=["T2"], writes=["T2s"])
                op("G", lambda e, qsl=qsl, bi=bi: e.tensor_add(
                    s_Qr[bi][:, qsl], s_T1[:], s_T2s[:]),
                   reads=["T1", "T2s"], writes=[f"Qr{bi}"])
            for kc in range(NTC):
                kpb, kbk = (p6, "p6") if kc % 2 == 0 else (p7, "p7")
                for c in range(DC):
                    op("T", lambda e, c=c, kc=kc, kpb=kpb, bi=bi: e.matmul(
                        kpb[:], s_wk[bi][:, c, :],
                        s_h1[:, c, 512 * kc:512 * (kc + 1)],
                        start=(c == 0), stop=(c == DC - 1),
                        skip_group_check=True),
                       reads=[f"wk{bi}", f"h1:{c}:{kc}"], writes=[kbk])
                ksl = slice(512 * kc, 512 * (kc + 1))
                op("V", lambda e, kpb=kpb, ksl=ksl: e.tensor_mul(
                    s_T1[:], kpb[:], s_cmap[:, ksl]),
                   reads=[kbk, "cmap"], writes=["T1"])
                op("V", lambda e, kpb=kpb, ksl=ksl: e.tensor_mul(
                    s_T2[:], kpb[:], s_smap[:, ksl]),
                   reads=[kbk, "smap"], writes=["T2"])
                for b4 in range(4):
                    o4 = slice(32 * b4, 32 * (b4 + 1))
                    x4 = slice(32 * (b4 ^ 1), 32 * ((b4 ^ 1) + 1))
                    op("V", lambda e, o4=o4, x4=x4: e.tensor_copy(
                        s_T2s[o4, :], s_T2[x4, :]),
                       reads=["T2"], writes=["T2s"])
                op("G", lambda e, ksl=ksl, bi=bi: e.tensor_add(
                    s_Kr[bi][:, ksl], s_T1[:], s_T2s[:]),
                   reads=["T1", "T2s"], writes=[f"Kr{bi}"])
            for tt in range(32):
                vpb, vbk = (p6, "p6") if tt % 2 == 0 else (p7, "p7")
                tsl = slice(128 * tt, 128 * (tt + 1))
                for c in range(DC):
                    op("T", lambda e, c=c, tsl=tsl, vpb=vpb, bi=bi: e.matmul(
                        vpb[:, 0:128], s_h1[:, c, tsl], s_wv[bi][:, c, :],
                        start=(c == 0), stop=(c == DC - 1),
                        skip_group_check=True),
                       reads=[f"wv{bi}", f"h1:{c}:{tt // 4}"], writes=[vbk])
                op("V", lambda e, tt=tt, vpb=vpb, bi=bi: e.tensor_copy(
                    s_Vt[bi][:, tt, 0:64], vpb[:, 0:64]),
                   reads=[vbk], writes=[f"Vt{bi}"])
                op("V", lambda e, tt=tt, vpb=vpb, bi=bi: e.tensor_copy(
                    s_Vt[bi][:, tt, 65:129], vpb[:, 64:128]),
                   reads=[vbk], writes=[f"Vt{bi}"])
            return sc.end_capture()

        nxt = p2_ops(0, 0)
        sc.ops.extend(nxt)
        for p in range(NPAIR):
            bi = p % 2
            nxt = p2_ops(p + 1, 1 - bi) if p + 1 < NPAIR else []
            ni = 0

            def drip(k):
                nonlocal ni
                while ni < len(nxt) and k > 0:
                    sc.ops.append(nxt[ni])
                    ni += 1
                    k -= 1

            if DEBUG and p == 0:
                op("Y", lambda e: e.dma_start(out=dbg["dbg_Qr"],
                                              in_=s_Qr[0][:]),
                   reads=["Qr0"], writes=["dbg2"], dma=True)
                op("Y", lambda e: e.dma_start(out=dbg["dbg_Kr"],
                                              in_=s_Kr[0][:]),
                   reads=["Kr0"], writes=["dbg3"], dma=True)
                op("Y", lambda e: e.dma_start(out=dbg["dbg_Vt"],
                                              in_=s_Vt[0][:]),
                   reads=["Vt0"], writes=["dbg4"], dma=True)
            pend = []
            for qc in range(NQC):
                if qc > 0 and pend:
                    pass
                pvA, pvAk = p4, "p4"
                pvB, pvBk = p5, "p5"
                qsl = slice(512 * qc, 512 * (qc + 1))

                def emit_st(kt, qsl=qsl, bi=bi):
                    par = kt % 2
                    stA = pA[:, 512 * par:512 * (par + 1)]
                    stB = pB[:, 512 * par:512 * (par + 1)]
                    ktsl = slice(128 * kt, 128 * (kt + 1))
                    op("T", lambda e, stA=stA, ktsl=ktsl, qsl=qsl, bi=bi:
                       e.matmul(stA, s_Kr[bi][0:64, ktsl],
                                s_Qr[bi][0:64, qsl],
                                start=True, stop=True,
                                tile_position=(0, 0),
                                skip_group_check=True),
                       reads=[f"Kr{bi}", f"Qr{bi}"], writes=[f"pA{par}"])
                    op("T", lambda e, stB=stB, ktsl=ktsl, qsl=qsl, bi=bi:
                       e.matmul(stB, s_Kr[bi][64:128, ktsl],
                                s_Qr[bi][64:128, qsl],
                                start=True, stop=True,
                                tile_position=(64, 0),
                                skip_group_check=True),
                       reads=[f"Kr{bi}", f"Qr{bi}"], writes=[f"pB{par}"])

                emit_st(0)
                for kt in range(32):
                    par = kt % 2
                    stA = pA[:, 512 * par:512 * (par + 1)]
                    stB = pB[:, 512 * par:512 * (par + 1)]
                    ptA, ptB = s_PtA[par], s_PtB[par]
                    pkA, pkB = f"PtA{par}", f"PtB{par}"
                    st = (kt == 0)
                    sp = (kt == 31)
                    op("S", lambda e, stA=stA, ptA=ptA: e.activation(
                        ptA[:], stA, ACT.Exp, scale=0.125),
                       reads=[f"pA{par}"], writes=[pkA])
                    op("S", lambda e, stB=stB, ptB=ptB: e.activation(
                        ptB[:], stB, ACT.Exp, scale=0.125),
                       reads=[f"pB{par}"], writes=[pkB])
                    if kt < 31:
                        emit_st(kt + 1)
                    op("T", lambda e, kt=kt, ptA=ptA, pvA=pvA, st=st,
                       sp=sp, bi=bi: e.matmul(
                           pvA[0:65, :], s_Vt[bi][:, kt, 0:65],
                           ptA[:], start=st, stop=sp,
                           skip_group_check=True),
                       reads=[f"Vt{bi}", pkA], writes=[pvAk])
                    op("T", lambda e, kt=kt, ptB=ptB, pvB=pvB, st=st,
                       sp=sp, bi=bi: e.matmul(
                           pvB[0:65, :], s_Vt[bi][:, kt, 65:130],
                           ptB[:], start=st, stop=sp,
                           skip_group_check=True),
                       reads=[f"Vt{bi}", pkB], writes=[pvBk])
                    drip(8)
                    if kt == 4 and pend:
                        sc.ops.extend(pend)
                        pend = []
                qpi = qc % 2
                op("V", lambda e, pvA=pvA: e.tensor_copy(
                    s_l2[0:1, 0:512], pvA[64:65, :]),
                   reads=[pvAk], writes=["l2"])
                op("V", lambda e, pvB=pvB: e.tensor_copy(
                    s_l2[0:1, 512:1024], pvB[64:65, :]),
                   reads=[pvBk], writes=["l2"])
                op("S", lambda e, pvA=pvA, qpi=qpi: e.copy(
                    s_pvS[qpi][0:64, :], pvA[0:64, :]),
                   reads=[pvAk], writes=[f"pvS{qpi}"])
                op("S", lambda e, pvB=pvB, qpi=qpi: e.copy(
                    s_pvS[qpi][64:128, :], pvB[0:64, :]),
                   reads=[pvBk], writes=[f"pvS{qpi}"])
                op("V", lambda e: e.reciprocal(s_l2r[:], s_l2[:]),
                   reads=["l2"], writes=["l2r"])
                sc.begin_capture()
                op("T", lambda e: e.matmul(
                    pB[0:64, 0:512], s_onesk[0:1, 0:64], s_l2r[0:1, 0:512],
                    start=True, stop=True, tile_position=(0, 0),
                    skip_group_check=True),
                   reads=["onesk", "l2r"], writes=["pB0"])
                op("T", lambda e: e.matmul(
                    pB[64:128, 0:512], s_onesk[0:1, 0:64],
                    s_l2r[0:1, 512:1024],
                    start=True, stop=True, tile_position=(0, 64),
                    skip_group_check=True),
                   reads=["onesk", "l2r"], writes=["pB0"])
                op("V", lambda e: e.tensor_copy(s_tmp1[:], pB[:, 0:512]),
                   reads=["pB0"], writes=["tmp1"])
                op("V", lambda e, qsl=qsl, p=p, qpi=qpi: e.tensor_mul(
                    s_O[:, p, qsl], s_pvS[qpi][:], s_tmp1[:]),
                   reads=[f"pvS{qpi}", "tmp1"], writes=[f"O:{p}"])
                pend.extend(sc.end_capture())
            if pend:
                sc.ops.extend(pend)
                pend = []
            drip(10 ** 9)

        es1.close()
        sc.barrier()

        es2b = contextlib.ExitStack()

        def sbR(name, shape, dtype):
            return es2b.enter_context(
                nc.sbuf_tensor(name, shape, dtype, side="right"))

        s_h2 = sbR("s_h2", [128, DC, TOK], F16)
        s_hmax = sbR("s_hmax", [128, 1], F32)
        s_h2s = sbR("s_h2s", [128, 1], F32)
        s_h2r = sbR("s_h2r", [128, 1], F32)
        es2 = contextlib.ExitStack()
        sb2 = mk(es2)
        s_wo = sb2("s_wo", [128, DC, D], BF16)
        s_stg2 = sb2("s_stage2", [128, D], F32)
        s_x2 = sb2("s_x2", [128, DC, TOK], F32)
        s_xb2 = sb2("s_xb2", [128, DC, 512], BF16)
        s_xsq2 = sb2("s_xsq2", [128, 512], BF16)
        s_xr = sb2("s_xr", [128, 512], F32)

        for c in range(DC):
            op("Y", lambda e, c=c: e.dma_start(
                out=s_stg2[:], in_=wo_h[128 * c:128 * (c + 1), :]),
               writes=["stage2"], dma=True)
            op("V", lambda e, c=c: e.tensor_copy(s_wo[:, c, :], s_stg2[:]),
               reads=["stage2"], writes=["wo"])
        op("V", lambda e: e.memset(s_hmax[:], 0.0), writes=["hmax"])

        def ln_block2(xsrc, xap_f, xkeys, outap_f, outkey_f, prec=False):
            if not prec:
                op("V", lambda e, xsrc=xsrc: e.tensor_copy(s_xb2[:], xsrc),
                   reads=xkeys, writes=["xb2:0"])
            xbk = [f"xb2:{c if prec else 0}" for c in range(DC)]
            for c in range(DC):
                op("S", lambda e, c=c: e.square(s_xsq2[:], s_xb2[:, c, :]),
                   reads=[xbk[c]], writes=["xsq2"])
                op("T", lambda e, c=c: e.matmul(
                    p6[0:1, 0:512], s_ones1[:], s_xb2[:, c, :],
                    start=(c == 0), stop=(c == DC - 1), tile_position=(0, 0),
                    skip_group_check=True),
                   reads=["ones1", xbk[c]], writes=["p6"])
                op("T", lambda e, c=c: e.matmul(
                    p6[32:33, 0:512], s_ones1[:], s_xsq2[:],
                    start=(c == 0), stop=(c == DC - 1), tile_position=(0, 32),
                    skip_group_check=True),
                   reads=["ones1", "xsq2"], writes=["p6"])
            op("V", lambda e: e.tensor_scalar(
                s_m[:], p6[0:1, 0:512], 1.0 / D, None, ALU.mult),
               reads=["p6"], writes=["m"])
            op("V", lambda e: e.tensor_scalar(
                s_v1[:], p6[32:33, 0:512], 1.0 / D, None, ALU.mult),
               reads=["p6"], writes=["v1"])
            op("V", lambda e: e.tensor_mul(s_msq[:], s_m[:], s_m[:]),
               reads=["m"], writes=["msq"])
            op("V", lambda e: e.tensor_sub(s_v1[:], s_v1[:], s_msq[:]),
               reads=["msq"], writes=["v1"])
            op("V", lambda e: e.tensor_scalar(
                s_v1[:], s_v1[:], 1e-5, None, ALU.add), writes=["v1"])
            op("V", lambda e: e.reciprocal(s_rec[:], s_v1[:]),
               reads=["v1"], writes=["rec"])
            op("S", lambda e: e.sqrt(s_r[:], s_rec[:]),
               reads=["rec"], writes=["r"])
            op("T", lambda e: e.matmul(
                p7[:], s_onesk[:], s_m[:], start=True, stop=True),
               reads=["onesk", "m"], writes=["p7"])
            op("T", lambda e: e.matmul(
                pA[:, 0:512], s_onesk[:], s_r[:], start=True, stop=True),
               reads=["onesk", "r"], writes=["pA"])
            for c in range(DC):
                op("V", lambda e, c=c: e.tensor_sub(
                    s_tmp1[:], xap_f(c), p7[:]),
                   reads=[xkeys[c], "p7"], writes=["tmp1"])
                op("V", lambda e, c=c: e.tensor_mul(
                    outap_f(c), s_tmp1[:], pA[:, 0:512]),
                   reads=["tmp1", "pA"], writes=[outkey_f(c)])

        for qc in range(NQC):
            qsl = slice(512 * qc, 512 * (qc + 1))
            for fc in range(DC):
                ob, obk = (p4, "p4") if fc % 2 == 0 else (p5, "p5")
                for c in range(DC):
                    op("T", lambda e, c=c, fc=fc, qsl=qsl, ob=ob: e.matmul(
                        ob[:], s_wo[:, c, 128 * fc:128 * (fc + 1)],
                        s_O[:, c, qsl], start=(c == 0), stop=(c == DC - 1),
                        skip_group_check=True),
                       reads=["wo", f"O:{c}"], writes=[obk])
                op("Y", lambda e, fc=fc, qsl=qsl: e.dma_start(
                    out=s_xr[:], in_=xi[128 * fc:128 * (fc + 1), qsl]),
                   writes=["xr"], dma=True)
                op("V", lambda e, fc=fc, qsl=qsl, ob=ob: e.tensor_add(
                    s_x2[:, fc, qsl], ob[:], s_xr[:]),
                   reads=[obk, "xr"], writes=[f"x2:{fc}:{qc}"])
                op("V", lambda e, fc=fc, qsl=qsl: e.tensor_copy(
                    s_xb2[:, fc, :], s_x2[:, fc, qsl]),
                   reads=[f"x2:{fc}:{qc}"], writes=[f"xb2:{fc}"])
            ln_block2(s_x2[:, :, qsl], lambda c, qsl=qsl: s_x2[:, c, qsl],
                      [f"x2:{c}:{qc}" for c in range(DC)],
                      lambda c, qsl=qsl: s_h2[:, c, qsl],
                      lambda c, qc=qc: f"h2:{c}:{qc}", prec=True)
            for fc in range(DC):
                op("V", lambda e, fc=fc, qsl=qsl: e.reduce_max(
                    s_red[:], s_h2[:, fc, qsl], AX,
                    apply_absolute_value=True),
                   reads=[f"h2:{fc}:{qc}"], writes=["red"])
                op("V", lambda e: e.tensor_max(
                    s_hmax[:], s_hmax[:], s_red[:]),
                   reads=["red"], writes=["hmax"])
        for fc in range(DC):
            op("Y", lambda e, fc=fc: e.dma_start(
                out=x2sp[fc], in_=s_x2[:, fc, :]),
               reads=[f"x2:{fc}:{qq}" for qq in range(NQC)],
               writes=[f"x2s:{fc}"], dma=True)

        if DEBUG:
            op("Y", lambda e: e.dma_start(out=dbg["dbg_x2"], in_=s_x2[:]),
               reads=[f"x2:{fc}:{qq}" for fc in range(DC)
                      for qq in range(NQC)], writes=["dbg6"], dma=True)
        def treemax(buf, key):
            for j, st32 in enumerate((32, 64, 96)):
                op("V", lambda e, st32=st32, j=j, buf=buf: e.tensor_copy(
                    s_m3[:, j:j + 1], buf[st32:st32 + 32, :]),
                   reads=[key], writes=["m3"])
            for j in range(3):
                op("V", lambda e, j=j, buf=buf: e.tensor_max(
                    buf[0:32, :], buf[0:32, :], s_m3[:, j:j + 1]),
                   reads=["m3"], writes=[key])
            op("V", lambda e: e.memset(s_t32[:], 0.0), writes=["t32"])
            op("V", lambda e, buf=buf: e.tensor_copy(
                s_t32[0:32, 0:1], buf[0:32, :]), reads=[key], writes=["t32"])
            op("V", lambda e: e.transpose(s_t32b[:], s_t32[:]),
               reads=["t32"], writes=["t32b"])
            op("V", lambda e, buf=buf: e.reduce_max(
                buf[0:1, :], s_t32b[0:1, :], AX),
               reads=["t32b"], writes=[key])

        def scale_pair(buf, maxkey, sB, rB, sBk, rBk):
            op("V", lambda e, buf=buf: e.tensor_scalar(
                s_sc1[:], buf[0:1, :], 1.0 / QMAX, None, ALU.mult),
               reads=[maxkey], writes=["sc1"])
            op("V", lambda e: e.reciprocal(s_sc2[:], s_sc1[:]),
               reads=["sc1"], writes=["sc2"])
            op("T", lambda e: e.matmul(pB[:, 0:1], s_onesk[:], s_sc1[:],
                                       start=True, stop=True),
               reads=["onesk", "sc1"], writes=["pB"])
            op("T", lambda e: e.matmul(pB[:, 1:2], s_onesk[:], s_sc2[:],
                                       start=True, stop=True),
               reads=["onesk", "sc2"], writes=["pB"])
            op("V", lambda e, sB=sB: e.tensor_copy(sB[:], pB[:, 0:1]),
               reads=["pB"], writes=[sBk])
            op("V", lambda e, rB=rB: e.tensor_copy(rB[:], pB[:, 1:2]),
               reads=["pB"], writes=[rBk])

        es2.close()
        esO.close()
        sc.barrier()
        treemax(s_hmax, "hmax")
        scale_pair(s_hmax, "hmax", s_h2s, s_h2r, "h2sB", "h2rB")

        es3 = contextlib.ExitStack()
        sb3 = mk(es3)
        s_qi = [sb3(f"s_qi{i}", [128, 512], I32) for i in range(2)]
        s_bmax = sb3("s_bmax", [128, 96], F32)
        s_bs = sb3("s_bs", [128, 96], F32)
        s_brs = sb3("s_brs", [128, 96], F32)
        s_gate = sb3("s_gate", [128, FC, TOK], F16)
        s_up16 = sb3("s_up16", [128, TOK], F16)
        s_umax = sb3("s_umax", [128, 1], F32)
        s_gmax = sb3("s_gmax", [128, 1], F32)
        s_us = sb3("s_us", [128, 1], F32)
        s_ur = sb3("s_ur", [128, 1], F32)
        s_gs = sb3("s_gs", [128, 1], F32)
        s_gr = sb3("s_gr", [128, 1], F32)
        s_gs16 = sb3("s_gs16", [128, 1], F32)
        s_ts = sb3("s_ts", [128, 1], F32)
        s_tr = sb3("s_tr", [128, 1], F32)
        s_os = sb3("s_os", [128, 1], F32)
        s_or = sb3("s_or", [128, 1], F32)
        es_h2q = contextlib.ExitStack()
        s_h2q = mk(es_h2q)("s_h2q", [128, DC, TOK], BF16)
        for fc in range(DC):
            for qc in range(NQC):
                sl = slice(512 * qc, 512 * (qc + 1))
                pi = (fc * NQC + qc) % 2
                op("S", lambda e, fc=fc, sl=sl, pi=pi: e.mul(
                    s_qi[pi][:], s_h2[:, fc, sl], s_h2r[:]),
                   reads=[f"h2:{fc}:{qc}", "h2rB"], writes=[f"qi{pi}"])
                op("S", lambda e, fc=fc, sl=sl, pi=pi: e.mul(
                    s_h2q[:, fc, sl], s_qi[pi][:], s_h2s[:]),
                   reads=[f"qi{pi}", "h2sB"], writes=[f"h2q:{fc}:{qc}"])
        if DEBUG:
            op("Y", lambda e: e.dma_start(out=dbg["dbg_h2q"], in_=s_h2q[:]),
               reads=[f"h2q:{fc}:{qq}" for fc in range(DC)
                      for qq in range(NQC)], writes=["dbg7"], dma=True)
        es2b.close()
        sc.barrier()

        es3a = contextlib.ExitStack()
        sb3a = mk(es3a)
        s_sel8 = sb3a("s_sel8", [32, DC, 128], F32)
        s_rsTu = sb3a("s_rsTu", [32, FF], F32)
        s_rsTg = sb3a("s_rsTg", [32, FF], F32)
        s_sTu = sb3a("s_sTu", [32, FF], BF16)
        s_sTg = sb3a("s_sTg", [32, FF], BF16)
        s_sel8b = sb3a("s_sel8b", [32, DC, 128], BF16)
        s_wslab = [sb3a("s_wslab0", [128, DC, 128], F32)]
        s_qslab = sb3a("s_qslab", [128, DC, 128], I32)
        s_wub = [sb3a(f"s_wub{i}", [128, DC, 128], BF16) for i in range(3)]
        s_woi = s_wslab[0][:].rearrange("p c o -> p (c o)")

        def w_scales(oi_ap, n_out, n_in, rsT, rkey, bufs=None, recip=None):
            nb = n_in // 32
            if bufs is None:
                bufs = [(s_woi, "wsl0")]
            for ot in range(n_out // 128):
                buf, bkey = bufs[ot % len(bufs)]
                op("Y", lambda e, ot=ot, buf=buf, n_in=n_in: e.dma_start(
                    out=buf[:, 0:n_in],
                    in_=oi_ap[128 * ot:128 * (ot + 1), :]),
                   writes=[bkey], dma=True)
                r3 = buf[:, 0:n_in].rearrange("p (g s) -> p g s", s=32)
                op("V", lambda e, r3=r3, nb=nb: e.tensor_reduce(
                    s_bmax[:, 0:nb], r3, AX, ALU.max,
                    apply_absolute_value=True),
                   reads=[bkey], writes=["bmax"])
                op("V", lambda e, nb=nb: e.tensor_scalar(
                    s_bs[:, 0:nb], s_bmax[:, 0:nb], 1.0 / 7.0, None,
                    ALU.mult),
                   reads=["bmax"], writes=["bs"])
                op("V", lambda e, nb=nb: e.reciprocal(
                    s_brs[:, 0:nb], s_bs[:, 0:nb]),
                   reads=["bs"], writes=["brs"])
                for j in range(4):
                    for g in range(nb // 32):
                        op("V", lambda e, j=j, ot=ot, g=g, rsT=rsT:
                           e.transpose(
                               rsT[32 * g:32 * (g + 1),
                                   128 * ot + 32 * j:128 * ot + 32 * (j + 1)],
                               s_brs[32 * j:32 * (j + 1),
                                     32 * g:32 * (g + 1)]),
                           reads=["brs"], writes=[f"{rkey}:{ot}"])
                if recip is not None:
                    rT, rk = recip
                    op("V", lambda e, ot=ot, rT=rT, rsT=rsT: e.reciprocal(
                        rT[:, 128 * ot:128 * (ot + 1)],
                        rsT[:, 128 * ot:128 * (ot + 1)]),
                       reads=[f"{rkey}:{ot}"], writes=[f"{rk}:{ot}"])

        op("Y", lambda e: e.dma_start(out=s_sel8[:], in_=sel8_h),
           writes=["sel8"], dma=True)
        op("V", lambda e: e.tensor_copy(s_sel8b[:], s_sel8[:]),
           reads=["sel8"], writes=["sel8b"])
        w_scales(wup_oi, FF, D, s_rsTu, "rsTu", recip=(s_sTu, "sTu"))
        w_scales(wg_oi, FF, D, s_rsTg, "rsTg", recip=(s_sTg, "sTg"))

        def dequant_slab(w_tiled, rsT, sT, rkey, skey, oc, bi, wkey):
            slab = s_wslab[0]
            src = w_tiled[oc].rearrange("(c p) o -> p c o", p=128)
            op("Y", lambda e, src=src, slab=slab: e.dma_start(
                out=slab[:], in_=src), writes=["wsl0"], dma=True)
            for ic in range(DC):
                pb_, pof, pk_ = [(p6, 0, "p6a"), (p7, 0, "p7a"),
                                 (p6, 256, "p6b"), (p7, 256, "p7b")][ic % 4]
                op("T", lambda e, ic=ic, oc=oc, rsT=rsT, pb_=pb_, pof=pof:
                   e.matmul(
                    pb_[:, pof:pof + 128], s_sel8[:, ic, :],
                    rsT[:, 128 * oc:128 * (oc + 1)],
                    start=True, stop=True, skip_group_check=True),
                   reads=["sel8", f"{rkey}:{oc}"], writes=[pk_])
                op("T", lambda e, ic=ic, oc=oc, sT=sT, pb_=pb_, pof=pof:
                   e.matmul(
                    pb_[:, pof + 128:pof + 256], s_sel8b[:, ic, :],
                    sT[:, 128 * oc:128 * (oc + 1)],
                    start=True, stop=True, skip_group_check=True),
                   reads=["sel8b", f"{skey}:{oc}"], writes=[pk_])
                op("V", lambda e, ic=ic, slab=slab, pb_=pb_, pof=pof:
                   e.tensor_tensor(
                    s_qslab[:, ic, :], slab[:, ic, :], pb_[:, pof:pof + 128],
                    ALU.mult),
                   reads=["wsl0", pk_], writes=["qsl"])
                op("V", lambda e, ic=ic, bi=bi, pb_=pb_, pof=pof:
                   e.tensor_tensor(
                    s_wub[bi][:, ic, :], s_qslab[:, ic, :],
                    pb_[:, pof + 128:pof + 256],
                    ALU.mult),
                   reads=["qsl", pk_], writes=[wkey])

        def ffn_proj(w_tiled, rsT, sT, rkey, skey, out_f, maxbuf, maxkey,
                     spill, dbgtap=False):
            for oc in range(2):
                dequant_slab(w_tiled, rsT, sT, rkey, skey, oc, oc % 3,
                             f"wub{oc % 3}")
            for oc in range(FC):
                wkey = f"wub{oc % 3}"
                if oc + 2 < FC:
                    dequant_slab(w_tiled, rsT, sT, rkey, skey, oc + 2,
                                 (oc + 2) % 3, f"wub{(oc + 2) % 3}")
                if DEBUG and dbgtap and oc == 0:
                    op("Y", lambda e: e.dma_start(out=dbg["dbg_wub"],
                                                  in_=s_wub[0][:]),
                       reads=[wkey], writes=["dbg9"], dma=True)
                for qc in range(NQC):
                    slot = (oc * NQC + qc) % 4
                    ub, ubk = [(p4[:], "p4"), (p5[:], "p5"),
                               (pA[:, 0:512], "pA0"),
                               (pA[:, 512:1024], "pA1")][slot]
                    for ic in range(DC):
                        op("T", lambda e, ic=ic, qc=qc, oc=oc, ub=ub:
                           e.matmul(ub, s_wub[oc % 3][:, ic, :],
                                    s_h2q[:, ic, 512 * qc:512 * (qc + 1)],
                                    start=(ic == 0), stop=(ic == DC - 1),
                                    skip_group_check=True),
                           reads=[wkey, f"h2q:{ic}:{qc}"], writes=[ubk])
                    dst, dkey = out_f(oc, qc)
                    op("S", lambda e, ub=ub, dst=dst: e.copy(
                        dst, ub), reads=[ubk], writes=[dkey])
                    if qc == NQC - 1:
                        rsrc = s_up16[:] if spill else s_gate[:, oc, :]
                        rkeys = ([f"up16:{q2}" for q2 in range(NQC)] if spill
                                 else [f"g:{oc}:{q2}" for q2 in range(NQC)])
                        op("V", lambda e, rsrc=rsrc: e.reduce_max(
                            s_red[:], rsrc, AX, apply_absolute_value=True),
                           reads=rkeys, writes=["red"])
                        op("V", lambda e, maxbuf=maxbuf: e.tensor_max(
                            maxbuf[:], maxbuf[:], s_red[:]),
                           reads=["red"], writes=[maxkey])
                        if spill:
                            op("Y", lambda e, oc=oc: e.dma_start(
                                out=upsp[oc], in_=s_up16[:]),
                               reads=rkeys, writes=[f"us:{oc}"], dma=True)

        op("V", lambda e: e.memset(s_umax[:], 0.0), writes=["umax"])
        op("V", lambda e: e.memset(s_gmax[:], 0.0), writes=["gmax"])
        ffn_proj(wupt, s_rsTu, s_sTu, "rsTu", "sTu",
                 lambda oc, qc: (s_up16[:, 512 * qc:512 * (qc + 1)],
                                 f"up16:{qc}"), s_umax, "umax", True,
                 dbgtap=True)
        ffn_proj(wgt, s_rsTg, s_sTg, "rsTg", "sTg",
                 lambda oc, qc: (s_gate[:, oc, 512 * qc:512 * (qc + 1)],
                                 f"g:{oc}:{qc}"), s_gmax, "gmax", False)

        es3a.close()
        es_h2q.close()
        sc.barrier()
        if DEBUG:
            op("Y", lambda e: e.dma_start(out=dbg["dbg_gate"], in_=s_gate[:]),
               reads=[f"g:{oc}:{qq}" for oc in range(FC)
                      for qq in range(NQC)], writes=["dbg8"], dma=True)
        treemax(s_umax, "umax")
        scale_pair(s_umax, "umax", s_us, s_ur, "usB", "urB")
        treemax(s_gmax, "gmax")
        scale_pair(s_gmax, "gmax", s_gs, s_gr, "gsB", "grB")
        op("V", lambda e: e.tensor_scalar(
            s_gs16[:], s_gs[:], 1.0 / 65536.0, None, ALU.mult),
           reads=["gsB"], writes=["gs16B"])

        es3d = contextlib.ExitStack()
        sb3d = mk(es3d)
        s_sel32 = sb3d("s_sel32", [32, 8, 128], F32)
        s_rg = [sb3d(f"s_rg{i}", [32, 128], F32) for i in range(2)]
        s_sg = [sb3d(f"s_sg{i}", [32, 128], F32) for i in range(2)]
        s_rsTd = sb3d("s_rsTd", [96, D], F32)
        s_sTd = sb3d("s_sTd", [96, D], F32)
        es3d0 = contextlib.ExitStack()
        s_wdoi = [mk(es3d0)(f"s_wdoi{i}", [128, FF], F32) for i in range(2)]
        op("Y", lambda e: e.dma_start(out=s_sel32[:], in_=sel32_h),
           writes=["sel32"], dma=True)
        w_scales(wd_oi, D, FF, s_rsTd, "rsTd",
                 bufs=[(s_wdoi[0][:], "wdoi0"), (s_wdoi[1][:], "wdoi1")],
                 recip=(s_sTd, "sTd"))
        es3d0.close()
        sc.barrier()
        s_wdhf = [sb3d(f"s_wdhf{i}", [128, FC // 2, 128], F32)
                  for i in range(2)]
        s_wdb = [sb3d(f"s_wdb{i}", [128, FC, 128], BF16) for i in range(8)]
        s_qd = sb3d("s_qd", [128, 256], I32)
        for sl8 in range(DC):
            cols = slice(128 * sl8, 128 * (sl8 + 1))
            for hb in range(2):
                slab = s_wdhf[(2 * sl8 + hb) % 2]
                slk = f"wdhf{(2 * sl8 + hb) % 2}"
                src2 = wdt[sl8][1536 * hb:1536 * (hb + 1), :].rearrange(
                    "(c p) o -> p c o", p=128)
                op("Y", lambda e, src2=src2, slab=slab: e.dma_start(
                    out=slab[:], in_=src2), writes=[slk], dma=True)
                for ic4 in range(FC // 4):
                    ic0 = 12 * hb + 2 * ic4
                    g = ic0 // 8
                    pof = 256 * (ic4 % 2)
                    if ic0 % 8 == 0 and g > 0:
                        par = g % 2
                        op("V", lambda e, g=g, par=par, cols=cols, sl8=sl8:
                           e.tensor_copy(s_rg[par][:],
                                         s_rsTd[32 * g:32 * (g + 1), cols]),
                           reads=[f"rsTd:{sl8}"], writes=[f"rg{par}"])
                        op("V", lambda e, g=g, par=par, cols=cols, sl8=sl8:
                           e.tensor_copy(s_sg[par][:],
                                         s_sTd[32 * g:32 * (g + 1), cols]),
                           reads=[f"sTd:{sl8}"], writes=[f"sg{par}"])
                    if g == 0:
                        rmv = s_rsTd[0:32, cols]
                        smv = s_sTd[0:32, cols]
                        rks, sks = f"rsTd:{sl8}", f"sTd:{sl8}"
                    else:
                        rmv, smv = s_rg[g % 2][:], s_sg[g % 2][:]
                        rks, sks = f"rg{g % 2}", f"sg{g % 2}"
                    for k4 in range(2):
                        j = (ic0 + k4) % 8
                        op("T", lambda e, j=j, rmv=rmv, k4=k4, pof=pof:
                           e.matmul(
                            p6[:, pof + 128 * k4:pof + 128 * (k4 + 1)],
                            s_sel32[:, j, :],
                            rmv, start=True, stop=True,
                            skip_group_check=True),
                           reads=["sel32", rks], writes=["p6"])
                        op("T", lambda e, j=j, smv=smv, k4=k4, pof=pof:
                           e.matmul(
                            p7[:, pof + 128 * k4:pof + 128 * (k4 + 1)],
                            s_sel32[:, j, :],
                            smv, start=True, stop=True,
                            skip_group_check=True),
                           reads=["sel32", sks], writes=["p7"])
                    icl = 2 * ic4
                    slabf = slab[:].rearrange("p c o -> p (c o)")
                    wdbf = s_wdb[sl8][:].rearrange("p c o -> p (c o)")
                    op("V", lambda e, icl=icl, slabf=slabf, slk=slk, pof=pof:
                       e.tensor_tensor(
                        s_qd[:], slabf[:, 128 * icl:128 * (icl + 2)],
                        p6[:, pof:pof + 256],
                        ALU.mult),
                       reads=[slk, "p6"], writes=["qd"])
                    op("V", lambda e, ic0=ic0, wdbf=wdbf, pof=pof:
                       e.tensor_tensor(
                        wdbf[:, 128 * ic0:128 * (ic0 + 2)],
                        s_qd[:],
                        p7[:, pof:pof + 256],
                        ALU.mult),
                       reads=["qd", "p7"], writes=[f"wdb{sl8}"])

        es3c = contextlib.ExitStack()
        sb3c = mk(es3c)
        s_sig = [sb3c(f"s_sig{i}", [128, 512], F32) for i in range(2)]
        s_gq = [sb3c(f"s_gq{i}", [128, 512], F32) for i in range(2)]
        s_sqi = [sb3c(f"s_sqi{i}", [128, 512], I32) for i in range(2)]
        s_tmax = sb3c("s_tmax", [128, 1], F32)
        s_ts_ = None
        op("V", lambda e: e.memset(s_tmax[:], 0.0), writes=["tmax"])
        for oc in range(FC):
            for qc in range(NQC):
                sl = slice(512 * qc, 512 * (qc + 1))
                pi = (oc * NQC + qc) % 2
                op("S", lambda e, oc=oc, sl=sl, pi=pi: e.mul(
                    s_qi[pi][:], s_gate[:, oc, sl], s_gr[:]),
                   reads=[f"g:{oc}:{qc}", "grB"], writes=[f"qi{pi}"])
                op("S", lambda e, pi=pi: e.mul(
                    s_gq[pi][:], s_qi[pi][:], s_gs[:]),
                   reads=[f"qi{pi}", "gsB"], writes=[f"gq{pi}"])
                op("S", lambda e, pi=pi: e.activation(
                    s_sig[pi][:], s_gq[pi][:], ACT.Sigmoid),
                   reads=[f"gq{pi}"], writes=[f"sig{pi}"])
                op("S", lambda e, pi=pi: e.mul(
                    s_sqi[pi][:], s_sig[pi][:], 65536.0),
                   reads=[f"sig{pi}"], writes=[f"sqi{pi}"])
                op("V", lambda e, pi=pi: e.tensor_scalar(
                    s_sqi[pi][:], s_sqi[pi][:], 65535.0, None, ALU.min),
                   writes=[f"sqi{pi}"])
                op("V", lambda e, oc=oc, sl=sl, pi=pi: e.scalar_tensor_tensor(
                    s_gate[:, oc, sl], s_qi[pi][:], s_gs16[:], s_sqi[pi][:],
                    ALU.mult, ALU.mult),
                   reads=[f"qi{pi}", "gs16B", f"sqi{pi}"],
                   writes=[f"g:{oc}:{qc}"])
                op("V", lambda e, oc=oc, sl=sl: e.reduce_max(
                    s_red[:], s_gate[:, oc, sl], AX,
                    apply_absolute_value=True),
                   reads=[f"g:{oc}:{qc}"], writes=["red"])
                op("V", lambda e: e.tensor_max(s_tmax[:], s_tmax[:], s_red[:]),
                   reads=["red"], writes=["tmax"])
        treemax(s_tmax, "tmax")
        scale_pair(s_tmax, "tmax", s_ts, s_tr, "tsB", "trB")
        es3c.close()
        sc.barrier()

        es3c2 = contextlib.ExitStack()
        sb3c2 = mk(es3c2)
        s_ur16 = [sb3c2(f"s_ur16{i}", [128, 512], F16) for i in range(2)]
        s_uqi = [sb3c2(f"s_uqi{i}", [128, 512], I32) for i in range(2)]
        s_q4 = [sb3c2(f"s_q4{i}", [128, 512], I32) for i in range(2)]
        s_of = [sb3c2(f"s_of{i}", [128, 512], F32) for i in range(2)]
        s_omax = sb3c2("s_omax", [128, 1], F32)
        op("V", lambda e: e.memset(s_omax[:], 0.0), writes=["omax"])
        for oc in range(FC):
            for qc in range(NQC):
                sl = slice(512 * qc, 512 * (qc + 1))
                pi = (oc * NQC + qc) % 2
                op("Y", lambda e, oc=oc, qc=qc, pi=pi: e.dma_start(
                    out=s_ur16[pi][:],
                    in_=upsp[oc, :, 512 * qc:512 * (qc + 1)]),
                   reads=[f"us:{oc}"], writes=[f"ur16{pi}"], dma=True)
                op("S", lambda e, pi=pi: e.mul(
                    s_uqi[pi][:], s_ur16[pi][:], s_ur[:]),
                   reads=[f"ur16{pi}", "urB"], writes=[f"uqi{pi}"])
                op("S", lambda e, oc=oc, sl=sl, pi=pi: e.mul(
                    s_q4[pi][:], s_gate[:, oc, sl], s_tr[:]),
                   reads=[f"g:{oc}:{qc}", "trB"], writes=[f"q4{pi}"])
                op("V", lambda e, pi=pi: e.scalar_tensor_tensor(
                    s_of[pi][:], s_q4[pi][:], s_ts[:], s_uqi[pi][:],
                    ALU.mult, ALU.mult),
                   reads=[f"q4{pi}", "tsB", f"uqi{pi}"], writes=[f"of{pi}"])
                op("S", lambda e, oc=oc, sl=sl, pi=pi: e.mul(
                    s_gate[:, oc, sl], s_of[pi][:], s_us[:]),
                   reads=[f"of{pi}", "usB"], writes=[f"g:{oc}:{qc}"])
                op("V", lambda e, oc=oc, sl=sl: e.reduce_max(
                    s_red[:], s_gate[:, oc, sl], AX,
                    apply_absolute_value=True),
                   reads=[f"g:{oc}:{qc}"], writes=["red"])
                op("V", lambda e: e.tensor_max(s_omax[:], s_omax[:], s_red[:]),
                   reads=["red"], writes=["omax"])
        treemax(s_omax, "omax")
        scale_pair(s_omax, "omax", s_os, s_or, "osB", "orB")
        es3c2.close()
        sc.barrier()
        es3p = contextlib.ExitStack()
        sb3p = mk(es3p)
        s_oq = [sb3p(f"s_oq{i}", [128, 512], BF16) for i in range(2)]
        s_x2r = [sb3p(f"s_x2r{i}", [128, 512], F32) for i in range(4)]
        s_x3 = [sb3p(f"s_x3{i}", [128, 512], F32) for i in range(2)]

        banks = [pA[:, 0:512], pA[:, 512:1024], pB[:, 0:512], pB[:, 512:1024],
                 p4[:], p5[:], p6[:], p7[:]]
        bkeys = ["pA", "pA", "pB", "pB", "p4", "p5", "p6", "p7"]
        def x2r_dma(fc, qc):
            op("Y", lambda e, fc=fc, qc=qc: e.dma_start(
                out=s_x2r[fc % 4][:],
                in_=x2sp[fc, :, 512 * qc:512 * (qc + 1)]),
               reads=[f"x2s:{fc}"], writes=[f"x2r{fc % 4}"], dma=True)

        for qc in range(NQC):
            for fc in range(4):
                x2r_dma(fc, qc)
            for ic in range(FC):
                sl = slice(512 * qc, 512 * (qc + 1))
                pi = ic % 2
                op("S", lambda e, ic=ic, sl=sl, pi=pi: e.mul(
                    s_qi[pi][:], s_gate[:, ic, sl], s_or[:]),
                   reads=[f"g:{ic}:{qc}", "orB"], writes=[f"qi{pi}"])
                op("S", lambda e, pi=pi: e.mul(
                    s_oq[pi][:], s_qi[pi][:], s_os[:]),
                   reads=[f"qi{pi}", "osB"], writes=[f"oq{pi}"])
                for gi in range(DC):
                    op("T", lambda e, ic=ic, gi=gi, pi=pi: e.matmul(
                        banks[gi], s_wdb[gi][:, ic, :], s_oq[pi][:],
                        start=(ic == 0), stop=(ic == FC - 1),
                        skip_group_check=True),
                       reads=[f"wdb{gi}", f"oq{pi}"], writes=[bkeys[gi]])
            for fc in range(DC):
                xi2 = fc % 2
                op("V", lambda e, fc=fc, xi2=xi2: e.tensor_add(
                    s_x3[xi2][:], banks[fc], s_x2r[fc % 4][:]),
                   reads=[bkeys[fc], f"x2r{fc % 4}"], writes=[f"x3{xi2}"])
                if fc + 4 < DC:
                    x2r_dma(fc + 4, qc)
                op("Y", lambda e, fc=fc, qc=qc, xi2=xi2: e.dma_start(
                    out=yo[128 * fc:128 * (fc + 1),
                           512 * qc:512 * (qc + 1)], in_=s_x3[xi2][:]),
                   reads=[f"x3{xi2}"], writes=[f"yo:{fc}:{qc}"], dma=True)
        es3p.close()
        es3d.close()
        es3.close()

        planned = sc.plan()
        per = {k: [] for k in "TVSGY"}
        for ek, fn, waits, amt in planned:
            per[ek].append((fn, waits, amt))
        ndma_i = [0]

        @blk.sync
        def _(sync):
            with nc.allow_low_precision(reason="bf16/f16 staging by design"):
                for fn, waits, amt in per["Y"]:
                    for s, v in waits:
                        sync.wait_ge(sems[s], v)
                    slot = f"Y{ndma_i[0] % 16}"
                    ndma_i[0] += 1
                    fn(sync).then_inc(sems[slot], 16)
                for s, v in sc.final_dma.items():
                    if v:
                        sync.wait_ge(sems[s], v)

        @blk.tensor
        def _(tensor):
            with nc.allow_low_precision(reason="bf16/f16 staging by design"):
                for fn, waits, amt in per["T"]:
                    for s, v in waits:
                        tensor.wait_ge(sems[s], v)
                    fn(tensor).then_inc(sems["T"], amt)

        @blk.vector
        def _(vector):
            with nc.allow_low_precision(reason="bf16/f16 staging by design"):
                for fn, waits, amt in per["V"]:
                    for s, v in waits:
                        vector.wait_ge(sems[s], v)
                    fn(vector).then_inc(sems["V"], amt)

        @blk.scalar
        def _(scalar):
            with nc.allow_low_precision(reason="bf16/f16 staging by design"):
                for fn, waits, amt in per["S"]:
                    for s, v in waits:
                        scalar.wait_ge(sems[s], v)
                    fn(scalar).then_inc(sems["S"], amt)

        @blk.gpsimd
        def _(gpsimd):
            with nc.allow_low_precision(reason="bf16/f16 staging by design"):
                for fn, waits, amt in per["G"]:
                    for s, v in waits:
                        gpsimd.wait_ge(sems[s], v)
                    fn(gpsimd).then_inc(sems["G"], amt)

    return nc


_NC_CACHE = {}


def kernel(L0_attn_x, L0_attn_W_q, L0_attn_W_k, L0_attn_W_v, L0_attn_W_o,
           L0_ffn_w_gate, L0_ffn_w_up, L0_ffn_w_down):
    x = np.asarray(L0_attn_x, np.float32)
    perm = head_perm()
    wq_t = np.ascontiguousarray(np.asarray(L0_attn_W_q, np.float32)
                                .T[:, perm].reshape(D, NPAIR, 128)
                                .transpose(1, 0, 2))
    wk_t = np.ascontiguousarray(np.asarray(L0_attn_W_k, np.float32)
                                .T[:, perm].reshape(D, NPAIR, 128)
                                .transpose(1, 0, 2))
    wv_t = np.ascontiguousarray(np.asarray(L0_attn_W_v, np.float32)
                                .T.reshape(D, NPAIR, 128).transpose(1, 0, 2))
    wo_t = np.ascontiguousarray(np.asarray(L0_attn_W_o, np.float32).T)
    wup = np.ascontiguousarray(np.asarray(L0_ffn_w_up, np.float32))
    wg = np.ascontiguousarray(np.asarray(L0_ffn_w_gate, np.float32))
    wd = np.ascontiguousarray(np.asarray(L0_ffn_w_down, np.float32))
    wupt = np.ascontiguousarray(wup.T.reshape(D, FC, 128).transpose(1, 0, 2))
    wgt_a = np.ascontiguousarray(wg.T.reshape(D, FC, 128).transpose(1, 0, 2))
    wdt_a = np.ascontiguousarray(wd.T.reshape(FF, DC, 128).transpose(1, 0, 2))
    p128 = np.arange(128)
    sel8 = np.zeros((32, DC, 128), np.float32)
    for ic in range(DC):
        for bb in range(32):
            sel8[bb, ic, :] = (bb == 4 * ic + p128 // 32)
    sel32 = np.zeros((32, 8, 128), np.float32)
    for j in range(8):
        for bb in range(32):
            sel32[bb, j, :] = (bb == 4 * j + p128 // 32)


    if "nc" not in _NC_CACHE:
        _NC_CACHE["nc"] = build()
    nc = _NC_CACHE["nc"]

    in_maps = []
    for c in range(8):
        b, h = c // 2, c % 2
        roll = h * TOK
        xt = np.ascontiguousarray(np.roll(x[b].T, -roll, axis=1))
        cmap, smap = rope_maps(roll)
        in_maps.append({
            "xi": xt, "wq": wq_t, "wk": wk_t, "wv": wv_t, "wo": wo_t,
            "wupt": wupt, "wgt": wgt_a, "wdt": wdt_a,
            "wup_oi": wup, "wg_oi": wg, "wd_oi": wd,
            "cmap": cmap, "smap": smap, "sel8": sel8, "sel32": sel32,
        })
    res = run_bass_kernel_spmd(nc, in_maps, core_ids=list(range(8)))
    out = np.empty((B, L, D), np.float32)
    for c in range(8):
        b, h = c // 2, c % 2
        yo_ = np.asarray(res.results[c]["yo"])
        out[b, h * TOK:(h + 1) * TOK, :] = yo_.T
    return out


# version 45 (final)
# speedup vs baseline: 1.0488x; 1.0488x over previous
import math
import contextlib
import numpy as np
import ml_dtypes
import concourse.bass as bass
import concourse.mybir as mybir
from concourse.bass_utils import run_bass_kernel_spmd

F32 = mybir.dt.float32
F16 = mybir.dt.float16
BF16 = mybir.dt.bfloat16
I32 = mybir.dt.int32
AX = mybir.AxisListType.X
ALU = mybir.AluOpType
ACT = mybir.ActivationFunctionType

B, L, D, FF, H, DH = 4, 4096, 1024, 3072, 16, 64
TOK, TA = 2048, 4096
DC, FC = D // 128, FF // 128
NQC, NTC = TOK // 512, TA // 512
NPAIR = H // 2
QMAX = 32767.0
DEBUG = False


class Sched:

    def __init__(self):
        self.ops = []

    def op(self, ek, fn, reads=(), writes=(), dma=False):
        tgt = self._cap if self._cap is not None else self.ops
        tgt.append((ek, fn, tuple(reads), tuple(writes), dma))

    _cap = None

    def begin_capture(self):
        self._cap = []

    def end_capture(self):
        c = self._cap
        self._cap = None
        return c

    def barrier(self):
        self.ops.append(("BARRIER", None, (), (), False))

    def plan(self):
        NDMA = 16
        dma_slots = [f"Y{i}" for i in range(NDMA)]
        cnt = {k: 0 for k in list("TVSG") + dma_slots}
        know = {k: {} for k in list("TVSG") + ["Y"]}
        wtag, rtags = {}, {}
        barrier = {}
        seen_keys = set()
        planned = []
        ndma = 0
        for ek, fn, reads, writes, dma in self.ops:
            if ek == "BARRIER":
                barrier = dict(cnt)
                continue
            need = {}

            def add(dep):
                if dep is None:
                    return
                s, v = dep
                if s == ek and ek == "T":
                    return
                if know[ek].get(s, 0) >= v:
                    return
                if need.get(s, 0) < v:
                    need[s] = v

            for key in reads:
                add(wtag.get(key))
            for key in writes:
                if key not in seen_keys and barrier:
                    for s, v in barrier.items():
                        if v:
                            add((s, v))
                add(wtag.get(key))
                for dep in rtags.get(key, ()):
                    add(dep)
            if ek == "Y":
                assert dma
                slot = dma_slots[ndma % NDMA]
                cyc = ndma // NDMA
                ndma += 1
                if cyc > 0:
                    add((slot, 16 * cyc))
                waits = sorted(need.items())
                for s, v in waits:
                    know[ek][s] = v
                cnt[slot] += 16
                tag = (slot, cnt[slot])
            else:
                waits = sorted(need.items())
                for s, v in waits:
                    know[ek][s] = v
                cnt[ek] += 1
                tag = (ek, cnt[ek])
            for key in reads:
                rtags.setdefault(key, []).append(tag)
            for key in writes:
                seen_keys.add(key)
                wtag[key] = tag
                rtags[key] = []
            planned.append((ek, fn, waits, 16 if ek == "Y" else 1))
        self.final_dma = {s: cnt[s] for s in dma_slots}
        return planned


def rope_maps(roll):
    pos = (np.arange(TA, dtype=np.float64) + roll) % L
    freqs = np.exp(-math.log(10000.0)
                   * np.arange(0, DH, 2, dtype=np.float64) / DH)
    ang = pos[None, :] * freqs[:, None]
    c32, s32 = np.cos(ang), np.sin(ang)
    cmap = np.tile(c32, (4, 1))
    smap = np.concatenate([s32, -s32, s32, -s32], axis=0)
    return (np.ascontiguousarray(cmap).astype(np.float16),
            np.ascontiguousarray(smap).astype(np.float16))


def head_perm():
    p = []
    for hh in range(H):
        p.extend((hh * 64 + 2 * np.arange(32)).tolist())
        p.extend((hh * 64 + 2 * np.arange(32) + 1).tolist())
    return np.array(p)


def build():
    nc = bass.Bass("TRN2", target_bir_lowering=False)
    xi = nc.dram_tensor("xi", [D, TA], F32, kind="ExternalInput").ap()
    wq_h = nc.dram_tensor("wq", [NPAIR, D, 128], F32,
                          kind="ExternalInput").ap()
    wk_h = nc.dram_tensor("wk", [NPAIR, D, 128], F32,
                          kind="ExternalInput").ap()
    wv_h = nc.dram_tensor("wv", [NPAIR, D, 128], F32,
                          kind="ExternalInput").ap()
    wo_h = nc.dram_tensor("wo", [D, D], F32, kind="ExternalInput").ap()
    wupt = nc.dram_tensor("wupt", [FC, D, 128], F32, kind="ExternalInput").ap()
    wgt = nc.dram_tensor("wgt", [FC, D, 128], F32, kind="ExternalInput").ap()
    wdt = nc.dram_tensor("wdt", [DC, FF, 128], F32, kind="ExternalInput").ap()
    wup_oi = nc.dram_tensor("wup_oi", [FF, D], F32, kind="ExternalInput").ap()
    wg_oi = nc.dram_tensor("wg_oi", [FF, D], F32, kind="ExternalInput").ap()
    wd_oi = nc.dram_tensor("wd_oi", [D, FF], F32, kind="ExternalInput").ap()
    cmap_h = nc.dram_tensor("cmap", [128, TA], F16, kind="ExternalInput").ap()
    smap_h = nc.dram_tensor("smap", [128, TA], F16, kind="ExternalInput").ap()
    sel8_h = nc.dram_tensor("sel8", [32, DC, 128], F32,
                            kind="ExternalInput").ap()
    sel32_h = nc.dram_tensor("sel32", [32, 8, 128], F32,
                             kind="ExternalInput").ap()
    yo = nc.dram_tensor("yo", [D, TOK], F32, kind="ExternalOutput").ap()
    dbg = {}
    if DEBUG:
        for nm, shp, dt in (("dbg_h1", [128, DC, TA], BF16),
                            ("dbg_Qr", [128, TOK], BF16),
                            ("dbg_Kr", [128, TA], BF16),
                            ("dbg_Vt", [128, 32, 130], BF16),
                            ("dbg_O", [128, DC, TOK], BF16),
                            ("dbg_x2", [128, DC, TOK], F32),
                            ("dbg_h2q", [128, DC, TOK], BF16),
                            ("dbg_gate", [128, FC, TOK], F16),
                            ("dbg_wub", [128, DC, 128], BF16)):
            dbg[nm] = nc.dram_tensor(nm, shp, dt, kind="ExternalOutput").ap()
    upsp = nc.dram_tensor("upsp", [FC, 128, TOK], F16, kind="Internal").ap()
    x2sp = nc.dram_tensor("x2sp", [DC, 128, TOK], F32, kind="Internal").ap()

    es0 = contextlib.ExitStack()
    sc = Sched()
    op = sc.op
    with es0:
        def mk(stack):
            def sb(name, shape, dtype):
                return stack.enter_context(nc.sbuf_tensor(name, shape, dtype))
            return sb

        sb0 = mk(es0)
        s_ones1 = sb0("s_ones1", [128, 1], BF16)
        s_onesk = sb0("s_onesk", [1, 128], F32)

        s_m = sb0("s_m", [1, 512], F32)
        s_r = sb0("s_r", [1, 512], F32)
        s_v1 = sb0("s_v1", [1, 512], F32)
        s_msq = sb0("s_msq", [1, 512], F32)
        s_rec = sb0("s_rec", [1, 512], F32)
        s_sc1 = sb0("s_sc1", [1, 1], F32)
        s_sc2 = sb0("s_sc2", [1, 1], F32)
        s_red = sb0("s_red", [128, 1], F32)
        s_tmp1 = sb0("s_tmp1", [128, 512], F32)
        s_t32 = sb0("s_t32", [32, 32], F32)
        s_t32b = sb0("s_t32b", [32, 32], F32)
        s_m3 = sb0("s_m3", [32, 3], F32)
        s_red2 = sb0("s_red2", [128, 1], F32)
        s_zero1 = sb0("s_zero1", [128, 1], F32)
        esO = contextlib.ExitStack()
        s_O = mk(esO)("s_O", [128, DC, TOK], BF16)

        pA = es0.enter_context(nc.psum_tensor("pA", [128, 1024], F32))
        pB = es0.enter_context(nc.psum_tensor("pB", [128, 1024], F32))
        p4 = es0.enter_context(nc.psum_tensor("p4", [128, 512], F32))
        p5 = es0.enter_context(nc.psum_tensor("p5", [128, 512], F32))
        p6 = es0.enter_context(nc.psum_tensor("p6", [128, 512], F32))
        p7 = es0.enter_context(nc.psum_tensor("p7", [128, 512], F32))

        sems = {k: es0.enter_context(nc.semaphore(f"sem{k}"))
                for k in list("TVSG") + [f"Y{i}" for i in range(16)]}
        blk = es0.enter_context(nc.Block())

        op("V", lambda e: e.memset(s_ones1[:], 1.0), writes=["ones1"])
        op("V", lambda e: e.memset(s_zero1[:], 0.0), writes=["zero1"])
        op("V", lambda e: e.memset(s_onesk[:], 1.0), writes=["onesk"])


        es1 = contextlib.ExitStack()
        sb1 = mk(es1)
        s_cmap = sb1("s_cmap", [128, TA], F16)
        s_smap = sb1("s_smap", [128, TA], F16)
        s_wq = [sb1(f"s_wq{i}", [128, DC, 128], BF16) for i in range(2)]
        s_wk = [sb1(f"s_wk{i}", [128, DC, 128], BF16) for i in range(2)]
        s_wv = [sb1(f"s_wv{i}", [128, DC, 128], BF16) for i in range(2)]
        s_wst = [sb1(f"s_wst{i}", [128, DC, 128], F32) for i in range(2)]
        s_h1 = sb1("s_h1", [128, DC, TA], BF16)
        es1a = contextlib.ExitStack()
        sb1a = mk(es1a)
        s_xf = sb1a("s_xf", [128, DC, 512], F32)
        s_xb = sb1a("s_xb", [128, DC, 512], BF16)
        s_xsq = sb1a("s_xsq", [128, 512], BF16)
        s_stage = sb1a("s_stage", [128, D], F32)

        op("Y", lambda e: e.dma_start(out=s_cmap[:], in_=cmap_h),
           writes=["cmap"], dma=True)
        op("Y", lambda e: e.dma_start(out=s_smap[:], in_=smap_h),
           writes=["smap"], dma=True)

        def ln_block(xsrc, xap_f, xkeys, outap_f, outkey_f, mbank, mkey,
                     bbank1, bbank2, bkey1, bkey2):
            op("V", lambda e, xsrc=xsrc: e.tensor_copy(
                s_xb[:], xsrc), reads=xkeys, writes=["xb"])
            for c in range(DC):
                op("S", lambda e, c=c: e.square(s_xsq[:], s_xb[:, c, :]),
                   reads=["xb"], writes=["xsq"])
                op("T", lambda e, c=c, mbank=mbank: e.matmul(
                    mbank[0:1, 0:512], s_ones1[:], s_xb[:, c, :],
                    start=(c == 0), stop=(c == DC - 1), tile_position=(0, 0),
                    skip_group_check=True),
                   reads=["ones1", "xb"], writes=[mkey])
                op("T", lambda e, c=c, mbank=mbank: e.matmul(
                    mbank[32:33, 0:512], s_ones1[:], s_xsq[:],
                    start=(c == 0), stop=(c == DC - 1), tile_position=(0, 32),
                    skip_group_check=True),
                   reads=["ones1", "xsq"], writes=[mkey])
            op("V", lambda e, mbank=mbank: e.tensor_scalar(
                s_m[:], mbank[0:1, 0:512], 1.0 / D, None, ALU.mult),
               reads=[mkey], writes=["m"])
            op("V", lambda e, mbank=mbank: e.tensor_scalar(
                s_v1[:], mbank[32:33, 0:512], 1.0 / D, None, ALU.mult),
               reads=[mkey], writes=["v1"])
            op("V", lambda e: e.tensor_mul(s_msq[:], s_m[:], s_m[:]),
               reads=["m"], writes=["msq"])
            op("V", lambda e: e.tensor_sub(s_v1[:], s_v1[:], s_msq[:]),
               reads=["msq"], writes=["v1"])
            op("V", lambda e: e.tensor_scalar(
                s_v1[:], s_v1[:], 1e-5, None, ALU.add), writes=["v1"])
            op("V", lambda e: e.reciprocal(s_rec[:], s_v1[:]),
               reads=["v1"], writes=["rec"])
            op("S", lambda e: e.sqrt(s_r[:], s_rec[:]),
               reads=["rec"], writes=["r"])
            op("T", lambda e, bbank1=bbank1: e.matmul(
                bbank1[:, 0:512], s_onesk[:], s_m[:], start=True, stop=True),
               reads=["onesk", "m"], writes=[bkey1])
            op("T", lambda e, bbank2=bbank2: e.matmul(
                bbank2[:, 0:512], s_onesk[:], s_r[:], start=True, stop=True),
               reads=["onesk", "r"], writes=[bkey2])
            for c in range(DC):
                op("V", lambda e, c=c, bbank1=bbank1: e.tensor_sub(
                    s_tmp1[:], xap_f(c), bbank1[:, 0:512]),
                   reads=[xkeys[c], bkey1], writes=["tmp1"])
                op("V", lambda e, c=c, bbank2=bbank2: e.tensor_mul(
                    outap_f(c), s_tmp1[:], bbank2[:, 0:512]),
                   reads=["tmp1", bkey2], writes=[outkey_f(c)])

        for tc in range(NTC):
            src = xi[:, 512 * tc:512 * (tc + 1)].rearrange(
                "(c p) t -> p c t", p=128)
            op("Y", lambda e, src=src: e.dma_start(out=s_xf[:], in_=src),
               writes=["xf"], dma=True)
            ln_block(s_xf[:], lambda c: s_xf[:, c, :], ["xf"] * DC,
                     lambda c, tc=tc: s_h1[:, c, 512 * tc:512 * (tc + 1)],
                     lambda c, tc=tc: f"h1:{c}:{tc}",
                     p4, "p4", p5, p6, "p5", "p6")

        if DEBUG:
            op("Y", lambda e: e.dma_start(out=dbg["dbg_h1"], in_=s_h1[:]),
               reads=[f"h1:{c}:{t}" for c in range(DC) for t in range(NTC)],
               writes=["dbg1"], dma=True)
        es1a.close()
        sc.barrier()
        s_Qr = [sb1(f"s_Qr{i}", [128, TOK], BF16) for i in range(2)]
        s_Kr = [sb1(f"s_Kr{i}", [128, TA], BF16) for i in range(2)]
        s_Vt = [sb1(f"s_Vt{i}", [128, 32, 130], BF16) for i in range(2)]
        s_T1 = sb1("s_T1", [128, 512], F32)
        s_T2 = sb1("s_T2", [128, 512], F32)
        s_T2s = sb1("s_T2s", [128, 512], F32)
        s_pvS = [sb1(f"s_pvS{i}", [128, 512], F32) for i in range(2)]
        s_l2 = sb1("s_l2", [1, 1024], F32)
        s_l2r = sb1("s_l2r", [1, 1024], F32)
        s_PtA = [sb1(f"s_PtA{i}", [128, 512], BF16) for i in range(2)]
        s_PtB = [sb1(f"s_PtB{i}", [128, 512], BF16) for i in range(2)]

        for i in range(2):
            op("V", lambda e, i=i: e.memset(s_Vt[i][:, :, 64:65], 1.0),
               writes=[f"Vt{i}"])
            op("V", lambda e, i=i: e.memset(s_Vt[i][:, :, 129:130], 1.0),
               writes=[f"Vt{i}"])

        def p2_ops(p, bi):
            sc.begin_capture()
            for nm, hsrc, dstw in (("wq", wq_h, s_wq), ("wk", wk_h, s_wk),
                                   ("wv", wv_h, s_wv)):
                st_ = s_wst[0] if nm != "wv" else s_wst[1]
                stk = "wst0" if nm != "wv" else "wst1"
                srcap = hsrc[p].rearrange("(c p) o -> p c o", p=128)
                op("Y", lambda e, srcap=srcap, st_=st_: e.dma_start(
                    out=st_[:], in_=srcap), writes=[stk], dma=True)
                op("V", lambda e, dstw=dstw, st_=st_, bi=bi: e.tensor_copy(
                    dstw[bi][:], st_[:]),
                   reads=[stk], writes=[f"{nm}{bi}"])
            for qc in range(NQC):
                qpb, qbk = (p6, "p6") if qc % 2 == 0 else (p7, "p7")
                for c in range(DC):
                    op("T", lambda e, c=c, qc=qc, qpb=qpb, bi=bi: e.matmul(
                        qpb[:], s_wq[bi][:, c, :],
                        s_h1[:, c, 512 * qc:512 * (qc + 1)],
                        start=(c == 0), stop=(c == DC - 1),
                        skip_group_check=True),
                       reads=[f"wq{bi}", f"h1:{c}:{qc}"], writes=[qbk])
                qsl = slice(512 * qc, 512 * (qc + 1))
                op("V", lambda e, qpb=qpb, qsl=qsl: e.tensor_mul(
                    s_T1[:], qpb[:], s_cmap[:, qsl]),
                   reads=[qbk, "cmap"], writes=["T1"])
                op("V", lambda e, qpb=qpb, qsl=qsl: e.tensor_mul(
                    s_T2[:], qpb[:], s_smap[:, qsl]),
                   reads=[qbk, "smap"], writes=["T2"])
                for b4 in range(4):
                    o4 = slice(32 * b4, 32 * (b4 + 1))
                    x4 = slice(32 * (b4 ^ 1), 32 * ((b4 ^ 1) + 1))
                    op("V", lambda e, o4=o4, x4=x4: e.tensor_copy(
                        s_T2s[o4, :], s_T2[x4, :]),
                       reads=["T2"], writes=["T2s"])
                op("G", lambda e, qsl=qsl, bi=bi: e.tensor_add(
                    s_Qr[bi][:, qsl], s_T1[:], s_T2s[:]),
                   reads=["T1", "T2s"], writes=[f"Qr{bi}"])
            for kc in range(NTC):
                kpb, kbk = (p6, "p6") if kc % 2 == 0 else (p7, "p7")
                for c in range(DC):
                    op("T", lambda e, c=c, kc=kc, kpb=kpb, bi=bi: e.matmul(
                        kpb[:], s_wk[bi][:, c, :],
                        s_h1[:, c, 512 * kc:512 * (kc + 1)],
                        start=(c == 0), stop=(c == DC - 1),
                        skip_group_check=True),
                       reads=[f"wk{bi}", f"h1:{c}:{kc}"], writes=[kbk])
                ksl = slice(512 * kc, 512 * (kc + 1))
                op("V", lambda e, kpb=kpb, ksl=ksl: e.tensor_mul(
                    s_T1[:], kpb[:], s_cmap[:, ksl]),
                   reads=[kbk, "cmap"], writes=["T1"])
                op("V", lambda e, kpb=kpb, ksl=ksl: e.tensor_mul(
                    s_T2[:], kpb[:], s_smap[:, ksl]),
                   reads=[kbk, "smap"], writes=["T2"])
                for b4 in range(4):
                    o4 = slice(32 * b4, 32 * (b4 + 1))
                    x4 = slice(32 * (b4 ^ 1), 32 * ((b4 ^ 1) + 1))
                    op("V", lambda e, o4=o4, x4=x4: e.tensor_copy(
                        s_T2s[o4, :], s_T2[x4, :]),
                       reads=["T2"], writes=["T2s"])
                op("G", lambda e, ksl=ksl, bi=bi: e.tensor_add(
                    s_Kr[bi][:, ksl], s_T1[:], s_T2s[:]),
                   reads=["T1", "T2s"], writes=[f"Kr{bi}"])
            for tt in range(32):
                vpb, vbk = (p6, "p6") if tt % 2 == 0 else (p7, "p7")
                tsl = slice(128 * tt, 128 * (tt + 1))
                for c in range(DC):
                    op("T", lambda e, c=c, tsl=tsl, vpb=vpb, bi=bi: e.matmul(
                        vpb[:, 0:128], s_h1[:, c, tsl], s_wv[bi][:, c, :],
                        start=(c == 0), stop=(c == DC - 1),
                        skip_group_check=True),
                       reads=[f"wv{bi}", f"h1:{c}:{tt // 4}"], writes=[vbk])
                op("V", lambda e, tt=tt, vpb=vpb, bi=bi: e.tensor_copy(
                    s_Vt[bi][:, tt, 0:64], vpb[:, 0:64]),
                   reads=[vbk], writes=[f"Vt{bi}"])
                op("V", lambda e, tt=tt, vpb=vpb, bi=bi: e.tensor_copy(
                    s_Vt[bi][:, tt, 65:129], vpb[:, 64:128]),
                   reads=[vbk], writes=[f"Vt{bi}"])
            return sc.end_capture()

        nxt = p2_ops(0, 0)
        sc.ops.extend(nxt)
        for p in range(NPAIR):
            bi = p % 2
            nxt = p2_ops(p + 1, 1 - bi) if p + 1 < NPAIR else []
            ni = 0

            def drip(k):
                nonlocal ni
                while ni < len(nxt) and k > 0:
                    sc.ops.append(nxt[ni])
                    ni += 1
                    k -= 1

            if DEBUG and p == 0:
                op("Y", lambda e: e.dma_start(out=dbg["dbg_Qr"],
                                              in_=s_Qr[0][:]),
                   reads=["Qr0"], writes=["dbg2"], dma=True)
                op("Y", lambda e: e.dma_start(out=dbg["dbg_Kr"],
                                              in_=s_Kr[0][:]),
                   reads=["Kr0"], writes=["dbg3"], dma=True)
                op("Y", lambda e: e.dma_start(out=dbg["dbg_Vt"],
                                              in_=s_Vt[0][:]),
                   reads=["Vt0"], writes=["dbg4"], dma=True)
            pend = []
            for qc in range(NQC):
                if qc > 0 and pend:
                    pass
                pvA, pvAk = p4, "p4"
                pvB, pvBk = p5, "p5"
                qsl = slice(512 * qc, 512 * (qc + 1))

                def emit_st(kt, qsl=qsl, bi=bi):
                    par = kt % 2
                    stA = pA[:, 512 * par:512 * (par + 1)]
                    stB = pB[:, 512 * par:512 * (par + 1)]
                    ktsl = slice(128 * kt, 128 * (kt + 1))
                    op("T", lambda e, stA=stA, ktsl=ktsl, qsl=qsl, bi=bi:
                       e.matmul(stA, s_Kr[bi][0:64, ktsl],
                                s_Qr[bi][0:64, qsl],
                                start=True, stop=True,
                                tile_position=(0, 0),
                                skip_group_check=True),
                       reads=[f"Kr{bi}", f"Qr{bi}"], writes=[f"pA{par}"])
                    op("T", lambda e, stB=stB, ktsl=ktsl, qsl=qsl, bi=bi:
                       e.matmul(stB, s_Kr[bi][64:128, ktsl],
                                s_Qr[bi][64:128, qsl],
                                start=True, stop=True,
                                tile_position=(64, 0),
                                skip_group_check=True),
                       reads=[f"Kr{bi}", f"Qr{bi}"], writes=[f"pB{par}"])

                emit_st(0)
                for kt in range(32):
                    par = kt % 2
                    stA = pA[:, 512 * par:512 * (par + 1)]
                    stB = pB[:, 512 * par:512 * (par + 1)]
                    ptA, ptB = s_PtA[par], s_PtB[par]
                    pkA, pkB = f"PtA{par}", f"PtB{par}"
                    st = (kt == 0)
                    sp = (kt == 31)
                    op("S", lambda e, stA=stA, ptA=ptA: e.activation(
                        ptA[:], stA, ACT.Exp, scale=0.125),
                       reads=[f"pA{par}"], writes=[pkA])
                    op("S", lambda e, stB=stB, ptB=ptB: e.activation(
                        ptB[:], stB, ACT.Exp, scale=0.125),
                       reads=[f"pB{par}"], writes=[pkB])
                    if kt < 31:
                        emit_st(kt + 1)
                    op("T", lambda e, kt=kt, ptA=ptA, pvA=pvA, st=st,
                       sp=sp, bi=bi: e.matmul(
                           pvA[0:65, :], s_Vt[bi][:, kt, 0:65],
                           ptA[:], start=st, stop=sp,
                           skip_group_check=True),
                       reads=[f"Vt{bi}", pkA], writes=[pvAk])
                    op("T", lambda e, kt=kt, ptB=ptB, pvB=pvB, st=st,
                       sp=sp, bi=bi: e.matmul(
                           pvB[0:65, :], s_Vt[bi][:, kt, 65:130],
                           ptB[:], start=st, stop=sp,
                           skip_group_check=True),
                       reads=[f"Vt{bi}", pkB], writes=[pvBk])
                    drip(4)
                    if kt == 4 and pend:
                        sc.ops.extend(pend)
                        pend = []
                qpi = qc % 2
                op("V", lambda e, pvA=pvA: e.tensor_copy(
                    s_l2[0:1, 0:512], pvA[64:65, :]),
                   reads=[pvAk], writes=["l2"])
                op("V", lambda e, pvB=pvB: e.tensor_copy(
                    s_l2[0:1, 512:1024], pvB[64:65, :]),
                   reads=[pvBk], writes=["l2"])
                op("S", lambda e, pvA=pvA, qpi=qpi: e.copy(
                    s_pvS[qpi][0:64, :], pvA[0:64, :]),
                   reads=[pvAk], writes=[f"pvS{qpi}"])
                op("S", lambda e, pvB=pvB, qpi=qpi: e.copy(
                    s_pvS[qpi][64:128, :], pvB[0:64, :]),
                   reads=[pvBk], writes=[f"pvS{qpi}"])
                op("V", lambda e: e.reciprocal(s_l2r[:], s_l2[:]),
                   reads=["l2"], writes=["l2r"])
                sc.begin_capture()
                op("T", lambda e: e.matmul(
                    pB[0:64, 0:512], s_onesk[0:1, 0:64], s_l2r[0:1, 0:512],
                    start=True, stop=True, tile_position=(0, 0),
                    skip_group_check=True),
                   reads=["onesk", "l2r"], writes=["pB0"])
                op("T", lambda e: e.matmul(
                    pB[64:128, 0:512], s_onesk[0:1, 0:64],
                    s_l2r[0:1, 512:1024],
                    start=True, stop=True, tile_position=(0, 64),
                    skip_group_check=True),
                   reads=["onesk", "l2r"], writes=["pB0"])
                op("V", lambda e: e.tensor_copy(s_tmp1[:], pB[:, 0:512]),
                   reads=["pB0"], writes=["tmp1"])
                op("V", lambda e, qsl=qsl, p=p, qpi=qpi: e.tensor_mul(
                    s_O[:, p, qsl], s_pvS[qpi][:], s_tmp1[:]),
                   reads=[f"pvS{qpi}", "tmp1"], writes=[f"O:{p}"])
                pend.extend(sc.end_capture())
            if pend:
                sc.ops.extend(pend)
                pend = []
            drip(10 ** 9)

        es1.close()
        sc.barrier()

        es2b = contextlib.ExitStack()

        def sbR(name, shape, dtype):
            return es2b.enter_context(
                nc.sbuf_tensor(name, shape, dtype, side="right"))

        s_h2 = sbR("s_h2", [128, DC, TOK], F16)
        s_hmax = sbR("s_hmax", [128, 1], F32)
        s_h2s = sbR("s_h2s", [128, 1], F32)
        s_h2r = sbR("s_h2r", [128, 1], F32)
        es2 = contextlib.ExitStack()
        sb2 = mk(es2)
        s_wo = sb2("s_wo", [128, DC, D], BF16)
        s_stg2 = sb2("s_stage2", [128, D], F32)
        s_x2 = sb2("s_x2", [128, DC, TOK], F32)
        s_xb2 = sb2("s_xb2", [128, DC, 512], BF16)
        s_xsq2 = sb2("s_xsq2", [128, 512], BF16)
        s_xr = sb2("s_xr", [128, 512], F32)

        for c in range(DC):
            op("Y", lambda e, c=c: e.dma_start(
                out=s_stg2[:], in_=wo_h[128 * c:128 * (c + 1), :]),
               writes=["stage2"], dma=True)
            op("V", lambda e, c=c: e.tensor_copy(s_wo[:, c, :], s_stg2[:]),
               reads=["stage2"], writes=["wo"])
        op("V", lambda e: e.memset(s_hmax[:], 0.0), writes=["hmax"])

        def ln_block2(xsrc, xap_f, xkeys, outap_f, outkey_f, prec=False):
            if not prec:
                op("V", lambda e, xsrc=xsrc: e.tensor_copy(s_xb2[:], xsrc),
                   reads=xkeys, writes=["xb2:0"])
            xbk = [f"xb2:{c if prec else 0}" for c in range(DC)]
            for c in range(DC):
                op("S", lambda e, c=c: e.square(s_xsq2[:], s_xb2[:, c, :]),
                   reads=[xbk[c]], writes=["xsq2"])
                op("T", lambda e, c=c: e.matmul(
                    p6[0:1, 0:512], s_ones1[:], s_xb2[:, c, :],
                    start=(c == 0), stop=(c == DC - 1), tile_position=(0, 0),
                    skip_group_check=True),
                   reads=["ones1", xbk[c]], writes=["p6"])
                op("T", lambda e, c=c: e.matmul(
                    p6[32:33, 0:512], s_ones1[:], s_xsq2[:],
                    start=(c == 0), stop=(c == DC - 1), tile_position=(0, 32),
                    skip_group_check=True),
                   reads=["ones1", "xsq2"], writes=["p6"])
            op("V", lambda e: e.tensor_scalar(
                s_m[:], p6[0:1, 0:512], 1.0 / D, None, ALU.mult),
               reads=["p6"], writes=["m"])
            op("V", lambda e: e.tensor_scalar(
                s_v1[:], p6[32:33, 0:512], 1.0 / D, None, ALU.mult),
               reads=["p6"], writes=["v1"])
            op("V", lambda e: e.tensor_mul(s_msq[:], s_m[:], s_m[:]),
               reads=["m"], writes=["msq"])
            op("V", lambda e: e.tensor_sub(s_v1[:], s_v1[:], s_msq[:]),
               reads=["msq"], writes=["v1"])
            op("V", lambda e: e.tensor_scalar(
                s_v1[:], s_v1[:], 1e-5, None, ALU.add), writes=["v1"])
            op("V", lambda e: e.reciprocal(s_rec[:], s_v1[:]),
               reads=["v1"], writes=["rec"])
            op("S", lambda e: e.sqrt(s_r[:], s_rec[:]),
               reads=["rec"], writes=["r"])
            op("T", lambda e: e.matmul(
                p7[:], s_onesk[:], s_m[:], start=True, stop=True),
               reads=["onesk", "m"], writes=["p7"])
            op("T", lambda e: e.matmul(
                pA[:, 0:512], s_onesk[:], s_r[:], start=True, stop=True),
               reads=["onesk", "r"], writes=["pA"])
            for c in range(DC):
                op("V", lambda e, c=c: e.tensor_sub(
                    s_tmp1[:], xap_f(c), p7[:]),
                   reads=[xkeys[c], "p7"], writes=["tmp1"])
                op("V", lambda e, c=c: e.tensor_mul(
                    outap_f(c), s_tmp1[:], pA[:, 0:512]),
                   reads=["tmp1", "pA"], writes=[outkey_f(c)])

        for qc in range(NQC):
            qsl = slice(512 * qc, 512 * (qc + 1))
            for fc in range(DC):
                ob, obk = (p4, "p4") if fc % 2 == 0 else (p5, "p5")
                for c in range(DC):
                    op("T", lambda e, c=c, fc=fc, qsl=qsl, ob=ob: e.matmul(
                        ob[:], s_wo[:, c, 128 * fc:128 * (fc + 1)],
                        s_O[:, c, qsl], start=(c == 0), stop=(c == DC - 1),
                        skip_group_check=True),
                       reads=["wo", f"O:{c}"], writes=[obk])
                op("Y", lambda e, fc=fc, qsl=qsl: e.dma_start(
                    out=s_xr[:], in_=xi[128 * fc:128 * (fc + 1), qsl]),
                   writes=["xr"], dma=True)
                op("V", lambda e, fc=fc, qsl=qsl, ob=ob: e.tensor_add(
                    s_x2[:, fc, qsl], ob[:], s_xr[:]),
                   reads=[obk, "xr"], writes=[f"x2:{fc}:{qc}"])
                op("V", lambda e, fc=fc, qsl=qsl: e.tensor_copy(
                    s_xb2[:, fc, :], s_x2[:, fc, qsl]),
                   reads=[f"x2:{fc}:{qc}"], writes=[f"xb2:{fc}"])
            ln_block2(s_x2[:, :, qsl], lambda c, qsl=qsl: s_x2[:, c, qsl],
                      [f"x2:{c}:{qc}" for c in range(DC)],
                      lambda c, qsl=qsl: s_h2[:, c, qsl],
                      lambda c, qc=qc: f"h2:{c}:{qc}", prec=True)
            for fc in range(DC):
                op("V", lambda e, fc=fc, qsl=qsl: e.reduce_max(
                    s_red[:], s_h2[:, fc, qsl], AX,
                    apply_absolute_value=True),
                   reads=[f"h2:{fc}:{qc}"], writes=["red"])
                op("V", lambda e: e.tensor_max(
                    s_hmax[:], s_hmax[:], s_red[:]),
                   reads=["red"], writes=["hmax"])
        for fc in range(DC):
            op("Y", lambda e, fc=fc: e.dma_start(
                out=x2sp[fc], in_=s_x2[:, fc, :]),
               reads=[f"x2:{fc}:{qq}" for qq in range(NQC)],
               writes=[f"x2s:{fc}"], dma=True)

        if DEBUG:
            op("Y", lambda e: e.dma_start(out=dbg["dbg_x2"], in_=s_x2[:]),
               reads=[f"x2:{fc}:{qq}" for fc in range(DC)
                      for qq in range(NQC)], writes=["dbg6"], dma=True)
        def treemax(buf, key):
            for j, st32 in enumerate((32, 64, 96)):
                op("V", lambda e, st32=st32, j=j, buf=buf: e.tensor_copy(
                    s_m3[:, j:j + 1], buf[st32:st32 + 32, :]),
                   reads=[key], writes=["m3"])
            for j in range(3):
                op("V", lambda e, j=j, buf=buf: e.tensor_max(
                    buf[0:32, :], buf[0:32, :], s_m3[:, j:j + 1]),
                   reads=["m3"], writes=[key])
            op("V", lambda e: e.memset(s_t32[:], 0.0), writes=["t32"])
            op("V", lambda e, buf=buf: e.tensor_copy(
                s_t32[0:32, 0:1], buf[0:32, :]), reads=[key], writes=["t32"])
            op("V", lambda e: e.transpose(s_t32b[:], s_t32[:]),
               reads=["t32"], writes=["t32b"])
            op("V", lambda e, buf=buf: e.reduce_max(
                buf[0:1, :], s_t32b[0:1, :], AX),
               reads=["t32b"], writes=[key])

        def scale_pair(buf, maxkey, sB, rB, sBk, rBk):
            op("V", lambda e, buf=buf: e.tensor_scalar(
                s_sc1[:], buf[0:1, :], 1.0 / QMAX, None, ALU.mult),
               reads=[maxkey], writes=["sc1"])
            op("V", lambda e: e.reciprocal(s_sc2[:], s_sc1[:]),
               reads=["sc1"], writes=["sc2"])
            op("T", lambda e: e.matmul(pB[:, 0:1], s_onesk[:], s_sc1[:],
                                       start=True, stop=True),
               reads=["onesk", "sc1"], writes=["pB"])
            op("T", lambda e: e.matmul(pB[:, 1:2], s_onesk[:], s_sc2[:],
                                       start=True, stop=True),
               reads=["onesk", "sc2"], writes=["pB"])
            op("V", lambda e, sB=sB: e.tensor_copy(sB[:], pB[:, 0:1]),
               reads=["pB"], writes=[sBk])
            op("V", lambda e, rB=rB: e.tensor_copy(rB[:], pB[:, 1:2]),
               reads=["pB"], writes=[rBk])

        es2.close()
        esO.close()
        sc.barrier()
        treemax(s_hmax, "hmax")
        scale_pair(s_hmax, "hmax", s_h2s, s_h2r, "h2sB", "h2rB")

        es3 = contextlib.ExitStack()
        sb3 = mk(es3)
        s_qi = [sb3(f"s_qi{i}", [128, 512], I32) for i in range(2)]
        s_bmax = sb3("s_bmax", [128, 96], F32)
        s_bs = sb3("s_bs", [128, 96], F32)
        s_brs = sb3("s_brs", [128, 96], F32)
        s_gate = sb3("s_gate", [128, FC, TOK], F16)
        s_up16 = sb3("s_up16", [128, TOK], F16)
        s_umax = sb3("s_umax", [128, 1], F32)
        s_gmax = sb3("s_gmax", [128, 1], F32)
        s_us = sb3("s_us", [128, 1], F32)
        s_ur = sb3("s_ur", [128, 1], F32)
        s_gs = sb3("s_gs", [128, 1], F32)
        s_gr = sb3("s_gr", [128, 1], F32)
        s_gs16 = sb3("s_gs16", [128, 1], F32)
        s_ts = sb3("s_ts", [128, 1], F32)
        s_tr = sb3("s_tr", [128, 1], F32)
        s_os = sb3("s_os", [128, 1], F32)
        s_or = sb3("s_or", [128, 1], F32)
        es_h2q = contextlib.ExitStack()
        s_h2q = mk(es_h2q)("s_h2q", [128, DC, TOK], BF16)
        for fc in range(DC):
            for qc in range(NQC):
                sl = slice(512 * qc, 512 * (qc + 1))
                pi = (fc * NQC + qc) % 2
                op("S", lambda e, fc=fc, sl=sl, pi=pi: e.mul(
                    s_qi[pi][:], s_h2[:, fc, sl], s_h2r[:]),
                   reads=[f"h2:{fc}:{qc}", "h2rB"], writes=[f"qi{pi}"])
                op("S", lambda e, fc=fc, sl=sl, pi=pi: e.mul(
                    s_h2q[:, fc, sl], s_qi[pi][:], s_h2s[:]),
                   reads=[f"qi{pi}", "h2sB"], writes=[f"h2q:{fc}:{qc}"])
        if DEBUG:
            op("Y", lambda e: e.dma_start(out=dbg["dbg_h2q"], in_=s_h2q[:]),
               reads=[f"h2q:{fc}:{qq}" for fc in range(DC)
                      for qq in range(NQC)], writes=["dbg7"], dma=True)
        es2b.close()
        sc.barrier()

        es3a = contextlib.ExitStack()
        sb3a = mk(es3a)
        s_sel8 = sb3a("s_sel8", [32, DC, 128], F32)
        s_rsTu = sb3a("s_rsTu", [32, FF], F32)
        s_rsTg = sb3a("s_rsTg", [32, FF], F32)
        s_sTu = sb3a("s_sTu", [32, FF], BF16)
        s_sTg = sb3a("s_sTg", [32, FF], BF16)
        s_sel8b = sb3a("s_sel8b", [32, DC, 128], BF16)
        s_wslab = [sb3a("s_wslab0", [128, DC, 128], F32)]
        s_qslab = sb3a("s_qslab", [128, DC, 128], I32)
        s_wub = [sb3a(f"s_wub{i}", [128, DC, 128], BF16) for i in range(3)]
        s_woi = s_wslab[0][:].rearrange("p c o -> p (c o)")

        def w_scales(oi_ap, n_out, n_in, rsT, rkey, bufs=None, recip=None):
            nb = n_in // 32
            if bufs is None:
                bufs = [(s_woi, "wsl0")]
            for ot in range(n_out // 128):
                buf, bkey = bufs[ot % len(bufs)]
                op("Y", lambda e, ot=ot, buf=buf, n_in=n_in: e.dma_start(
                    out=buf[:, 0:n_in],
                    in_=oi_ap[128 * ot:128 * (ot + 1), :]),
                   writes=[bkey], dma=True)
                r3 = buf[:, 0:n_in].rearrange("p (g s) -> p g s", s=32)
                op("V", lambda e, r3=r3, nb=nb: e.tensor_reduce(
                    s_bmax[:, 0:nb], r3, AX, ALU.max,
                    apply_absolute_value=True),
                   reads=[bkey], writes=["bmax"])
                op("V", lambda e, nb=nb: e.tensor_scalar(
                    s_bs[:, 0:nb], s_bmax[:, 0:nb], 1.0 / 7.0, None,
                    ALU.mult),
                   reads=["bmax"], writes=["bs"])
                op("V", lambda e, nb=nb: e.reciprocal(
                    s_brs[:, 0:nb], s_bs[:, 0:nb]),
                   reads=["bs"], writes=["brs"])
                for j in range(4):
                    for g in range(nb // 32):
                        op("V", lambda e, j=j, ot=ot, g=g, rsT=rsT:
                           e.transpose(
                               rsT[32 * g:32 * (g + 1),
                                   128 * ot + 32 * j:128 * ot + 32 * (j + 1)],
                               s_brs[32 * j:32 * (j + 1),
                                     32 * g:32 * (g + 1)]),
                           reads=["brs"], writes=[f"{rkey}:{ot}"])
                if recip is not None:
                    rT, rk = recip
                    op("V", lambda e, ot=ot, rT=rT, rsT=rsT: e.reciprocal(
                        rT[:, 128 * ot:128 * (ot + 1)],
                        rsT[:, 128 * ot:128 * (ot + 1)]),
                       reads=[f"{rkey}:{ot}"], writes=[f"{rk}:{ot}"])

        op("Y", lambda e: e.dma_start(out=s_sel8[:], in_=sel8_h),
           writes=["sel8"], dma=True)
        op("V", lambda e: e.tensor_copy(s_sel8b[:], s_sel8[:]),
           reads=["sel8"], writes=["sel8b"])
        w_scales(wup_oi, FF, D, s_rsTu, "rsTu", recip=(s_sTu, "sTu"))
        w_scales(wg_oi, FF, D, s_rsTg, "rsTg", recip=(s_sTg, "sTg"))

        def dequant_slab(w_tiled, rsT, sT, rkey, skey, oc, bi, wkey):
            slab = s_wslab[0]
            src = w_tiled[oc].rearrange("(c p) o -> p c o", p=128)
            op("Y", lambda e, src=src, slab=slab: e.dma_start(
                out=slab[:], in_=src), writes=["wsl0"], dma=True)
            for ic in range(DC):
                pb_, pof, pk_ = [(p6, 0, "p6a"), (p7, 0, "p7a"),
                                 (p6, 256, "p6b"), (p7, 256, "p7b")][ic % 4]
                op("T", lambda e, ic=ic, oc=oc, rsT=rsT, pb_=pb_, pof=pof:
                   e.matmul(
                    pb_[:, pof:pof + 128], s_sel8[:, ic, :],
                    rsT[:, 128 * oc:128 * (oc + 1)],
                    start=True, stop=True, skip_group_check=True),
                   reads=["sel8", f"{rkey}:{oc}"], writes=[pk_])
                op("T", lambda e, ic=ic, oc=oc, sT=sT, pb_=pb_, pof=pof:
                   e.matmul(
                    pb_[:, pof + 128:pof + 256], s_sel8b[:, ic, :],
                    sT[:, 128 * oc:128 * (oc + 1)],
                    start=True, stop=True, skip_group_check=True),
                   reads=["sel8b", f"{skey}:{oc}"], writes=[pk_])
                op("V", lambda e, ic=ic, slab=slab, pb_=pb_, pof=pof:
                   e.tensor_tensor(
                    s_qslab[:, ic, :], slab[:, ic, :], pb_[:, pof:pof + 128],
                    ALU.mult),
                   reads=["wsl0", pk_], writes=["qsl"])
                op("V", lambda e, ic=ic, bi=bi, pb_=pb_, pof=pof:
                   e.tensor_tensor(
                    s_wub[bi][:, ic, :], s_qslab[:, ic, :],
                    pb_[:, pof + 128:pof + 256],
                    ALU.mult),
                   reads=["qsl", pk_], writes=[wkey])

        def ffn_proj(w_tiled, rsT, sT, rkey, skey, out_f, maxbuf, maxkey,
                     spill, dbgtap=False):
            for oc in range(2):
                dequant_slab(w_tiled, rsT, sT, rkey, skey, oc, oc % 3,
                             f"wub{oc % 3}")
            for oc in range(FC):
                wkey = f"wub{oc % 3}"
                if oc + 2 < FC:
                    dequant_slab(w_tiled, rsT, sT, rkey, skey, oc + 2,
                                 (oc + 2) % 3, f"wub{(oc + 2) % 3}")
                if DEBUG and dbgtap and oc == 0:
                    op("Y", lambda e: e.dma_start(out=dbg["dbg_wub"],
                                                  in_=s_wub[0][:]),
                       reads=[wkey], writes=["dbg9"], dma=True)
                for qc in range(NQC):
                    slot = (oc * NQC + qc) % 4
                    ub, ubk = [(p4[:], "p4"), (p5[:], "p5"),
                               (pA[:, 0:512], "pA0"),
                               (pA[:, 512:1024], "pA1")][slot]
                    for ic in range(DC):
                        op("T", lambda e, ic=ic, qc=qc, oc=oc, ub=ub:
                           e.matmul(ub, s_wub[oc % 3][:, ic, :],
                                    s_h2q[:, ic, 512 * qc:512 * (qc + 1)],
                                    start=(ic == 0), stop=(ic == DC - 1),
                                    skip_group_check=True),
                           reads=[wkey, f"h2q:{ic}:{qc}"], writes=[ubk])
                    dst, dkey = out_f(oc, qc)
                    op("S", lambda e, ub=ub, dst=dst: e.copy(
                        dst, ub), reads=[ubk], writes=[dkey])
                    if qc == NQC - 1:
                        rsrc = s_up16[:] if spill else s_gate[:, oc, :]
                        rkeys = ([f"up16:{q2}" for q2 in range(NQC)] if spill
                                 else [f"g:{oc}:{q2}" for q2 in range(NQC)])
                        op("V", lambda e, rsrc=rsrc: e.reduce_max(
                            s_red[:], rsrc, AX, apply_absolute_value=True),
                           reads=rkeys, writes=["red"])
                        op("V", lambda e, maxbuf=maxbuf: e.tensor_max(
                            maxbuf[:], maxbuf[:], s_red[:]),
                           reads=["red"], writes=[maxkey])
                        if spill:
                            op("Y", lambda e, oc=oc: e.dma_start(
                                out=upsp[oc], in_=s_up16[:]),
                               reads=rkeys, writes=[f"us:{oc}"], dma=True)

        op("V", lambda e: e.memset(s_umax[:], 0.0), writes=["umax"])
        op("V", lambda e: e.memset(s_gmax[:], 0.0), writes=["gmax"])
        ffn_proj(wupt, s_rsTu, s_sTu, "rsTu", "sTu",
                 lambda oc, qc: (s_up16[:, 512 * qc:512 * (qc + 1)],
                                 f"up16:{qc}"), s_umax, "umax", True,
                 dbgtap=True)
        ffn_proj(wgt, s_rsTg, s_sTg, "rsTg", "sTg",
                 lambda oc, qc: (s_gate[:, oc, 512 * qc:512 * (qc + 1)],
                                 f"g:{oc}:{qc}"), s_gmax, "gmax", False)

        es3a.close()
        es_h2q.close()
        sc.barrier()
        if DEBUG:
            op("Y", lambda e: e.dma_start(out=dbg["dbg_gate"], in_=s_gate[:]),
               reads=[f"g:{oc}:{qq}" for oc in range(FC)
                      for qq in range(NQC)], writes=["dbg8"], dma=True)
        treemax(s_umax, "umax")
        scale_pair(s_umax, "umax", s_us, s_ur, "usB", "urB")
        treemax(s_gmax, "gmax")
        scale_pair(s_gmax, "gmax", s_gs, s_gr, "gsB", "grB")
        op("V", lambda e: e.tensor_scalar(
            s_gs16[:], s_gs[:], 1.0 / 65536.0, None, ALU.mult),
           reads=["gsB"], writes=["gs16B"])

        es3d = contextlib.ExitStack()
        sb3d = mk(es3d)
        s_sel32 = sb3d("s_sel32", [32, 8, 128], F32)
        s_rg = [sb3d(f"s_rg{i}", [32, 128], F32) for i in range(2)]
        s_sg = [sb3d(f"s_sg{i}", [32, 128], F32) for i in range(2)]
        s_rsTd = sb3d("s_rsTd", [96, D], F32)
        s_sTd = sb3d("s_sTd", [96, D], F32)
        es3d0 = contextlib.ExitStack()
        s_wdoi = [mk(es3d0)(f"s_wdoi{i}", [128, FF], F32) for i in range(2)]
        op("Y", lambda e: e.dma_start(out=s_sel32[:], in_=sel32_h),
           writes=["sel32"], dma=True)
        w_scales(wd_oi, D, FF, s_rsTd, "rsTd",
                 bufs=[(s_wdoi[0][:], "wdoi0"), (s_wdoi[1][:], "wdoi1")],
                 recip=(s_sTd, "sTd"))
        es3d0.close()
        sc.barrier()
        s_wdhf = [sb3d(f"s_wdhf{i}", [128, FC // 2, 128], F32)
                  for i in range(2)]
        s_wdb = [sb3d(f"s_wdb{i}", [128, FC, 128], BF16) for i in range(8)]
        s_qd = sb3d("s_qd", [128, 256], I32)
        for sl8 in range(DC):
            cols = slice(128 * sl8, 128 * (sl8 + 1))
            for hb in range(2):
                slab = s_wdhf[(2 * sl8 + hb) % 2]
                slk = f"wdhf{(2 * sl8 + hb) % 2}"
                src2 = wdt[sl8][1536 * hb:1536 * (hb + 1), :].rearrange(
                    "(c p) o -> p c o", p=128)
                op("Y", lambda e, src2=src2, slab=slab: e.dma_start(
                    out=slab[:], in_=src2), writes=[slk], dma=True)
                for ic4 in range(FC // 4):
                    ic0 = 12 * hb + 2 * ic4
                    g = ic0 // 8
                    pof = 256 * (ic4 % 2)
                    if ic0 % 8 == 0 and g > 0:
                        par = g % 2
                        op("V", lambda e, g=g, par=par, cols=cols, sl8=sl8:
                           e.tensor_copy(s_rg[par][:],
                                         s_rsTd[32 * g:32 * (g + 1), cols]),
                           reads=[f"rsTd:{sl8}"], writes=[f"rg{par}"])
                        op("V", lambda e, g=g, par=par, cols=cols, sl8=sl8:
                           e.tensor_copy(s_sg[par][:],
                                         s_sTd[32 * g:32 * (g + 1), cols]),
                           reads=[f"sTd:{sl8}"], writes=[f"sg{par}"])
                    if g == 0:
                        rmv = s_rsTd[0:32, cols]
                        smv = s_sTd[0:32, cols]
                        rks, sks = f"rsTd:{sl8}", f"sTd:{sl8}"
                    else:
                        rmv, smv = s_rg[g % 2][:], s_sg[g % 2][:]
                        rks, sks = f"rg{g % 2}", f"sg{g % 2}"
                    for k4 in range(2):
                        j = (ic0 + k4) % 8
                        op("T", lambda e, j=j, rmv=rmv, k4=k4, pof=pof:
                           e.matmul(
                            p6[:, pof + 128 * k4:pof + 128 * (k4 + 1)],
                            s_sel32[:, j, :],
                            rmv, start=True, stop=True,
                            skip_group_check=True),
                           reads=["sel32", rks], writes=["p6"])
                        op("T", lambda e, j=j, smv=smv, k4=k4, pof=pof:
                           e.matmul(
                            p7[:, pof + 128 * k4:pof + 128 * (k4 + 1)],
                            s_sel32[:, j, :],
                            smv, start=True, stop=True,
                            skip_group_check=True),
                           reads=["sel32", sks], writes=["p7"])
                    icl = 2 * ic4
                    slabf = slab[:].rearrange("p c o -> p (c o)")
                    wdbf = s_wdb[sl8][:].rearrange("p c o -> p (c o)")
                    op("V", lambda e, icl=icl, slabf=slabf, slk=slk, pof=pof:
                       e.tensor_tensor(
                        s_qd[:], slabf[:, 128 * icl:128 * (icl + 2)],
                        p6[:, pof:pof + 256],
                        ALU.mult),
                       reads=[slk, "p6"], writes=["qd"])
                    op("V", lambda e, ic0=ic0, wdbf=wdbf, pof=pof:
                       e.tensor_tensor(
                        wdbf[:, 128 * ic0:128 * (ic0 + 2)],
                        s_qd[:],
                        p7[:, pof:pof + 256],
                        ALU.mult),
                       reads=["qd", "p7"], writes=[f"wdb{sl8}"])

        es3c = contextlib.ExitStack()
        sb3c = mk(es3c)
        s_sig = [sb3c(f"s_sig{i}", [128, 512], F32) for i in range(2)]
        s_gq = [sb3c(f"s_gq{i}", [128, 512], F32) for i in range(2)]
        s_sqi = [sb3c(f"s_sqi{i}", [128, 512], I32) for i in range(2)]
        s_tmax = sb3c("s_tmax", [128, 1], F32)
        s_ts_ = None
        op("V", lambda e: e.memset(s_tmax[:], 0.0), writes=["tmax"])
        for oc in range(FC):
            for qc in range(NQC):
                sl = slice(512 * qc, 512 * (qc + 1))
                pi = (oc * NQC + qc) % 2
                op("S", lambda e, oc=oc, sl=sl, pi=pi: e.mul(
                    s_qi[pi][:], s_gate[:, oc, sl], s_gr[:]),
                   reads=[f"g:{oc}:{qc}", "grB"], writes=[f"qi{pi}"])
                op("S", lambda e, pi=pi: e.activation(
                    s_sig[pi][:], s_qi[pi][:], ACT.Sigmoid, scale=s_gs[:]),
                   reads=[f"qi{pi}", "gsB"], writes=[f"sig{pi}"])
                op("S", lambda e, pi=pi: e.mul(
                    s_sqi[pi][:], s_sig[pi][:], 65536.0),
                   reads=[f"sig{pi}"], writes=[f"sqi{pi}"])
                op("V", lambda e, pi=pi: e.tensor_scalar(
                    s_sqi[pi][:], s_sqi[pi][:], 65535.0, None, ALU.min),
                   writes=[f"sqi{pi}"])
                op("V", lambda e, oc=oc, sl=sl, pi=pi: e.scalar_tensor_tensor(
                    s_gate[:, oc, sl], s_qi[pi][:], s_gs16[:], s_sqi[pi][:],
                    ALU.mult, ALU.mult),
                   reads=[f"qi{pi}", "gs16B", f"sqi{pi}"],
                   writes=[f"g:{oc}:{qc}"])
                op("V", lambda e, oc=oc, sl=sl: e.reduce_max(
                    s_red[:], s_gate[:, oc, sl], AX,
                    apply_absolute_value=True),
                   reads=[f"g:{oc}:{qc}"], writes=["red"])
                op("V", lambda e: e.tensor_max(s_tmax[:], s_tmax[:], s_red[:]),
                   reads=["red"], writes=["tmax"])
        treemax(s_tmax, "tmax")
        scale_pair(s_tmax, "tmax", s_ts, s_tr, "tsB", "trB")
        es3c.close()
        sc.barrier()

        es3c2 = contextlib.ExitStack()
        sb3c2 = mk(es3c2)
        s_ur16 = [sb3c2(f"s_ur16{i}", [128, 512], F16) for i in range(2)]
        s_uqi = [sb3c2(f"s_uqi{i}", [128, 512], I32) for i in range(2)]
        s_q4 = [sb3c2(f"s_q4{i}", [128, 512], I32) for i in range(2)]
        s_of = [sb3c2(f"s_of{i}", [128, 512], F32) for i in range(2)]
        s_omax = sb3c2("s_omax", [128, 1], F32)
        op("V", lambda e: e.memset(s_omax[:], 0.0), writes=["omax"])
        for oc in range(FC):
            for qc in range(NQC):
                sl = slice(512 * qc, 512 * (qc + 1))
                pi = (oc * NQC + qc) % 2
                op("Y", lambda e, oc=oc, qc=qc, pi=pi: e.dma_start(
                    out=s_ur16[pi][:],
                    in_=upsp[oc, :, 512 * qc:512 * (qc + 1)]),
                   reads=[f"us:{oc}"], writes=[f"ur16{pi}"], dma=True)
                op("S", lambda e, pi=pi: e.mul(
                    s_uqi[pi][:], s_ur16[pi][:], s_ur[:]),
                   reads=[f"ur16{pi}", "urB"], writes=[f"uqi{pi}"])
                op("S", lambda e, oc=oc, sl=sl, pi=pi: e.mul(
                    s_q4[pi][:], s_gate[:, oc, sl], s_tr[:]),
                   reads=[f"g:{oc}:{qc}", "trB"], writes=[f"q4{pi}"])
                op("V", lambda e, pi=pi: e.scalar_tensor_tensor(
                    s_of[pi][:], s_q4[pi][:], s_ts[:], s_uqi[pi][:],
                    ALU.mult, ALU.mult),
                   reads=[f"q4{pi}", "tsB", f"uqi{pi}"], writes=[f"of{pi}"])
                op("S", lambda e, oc=oc, sl=sl, pi=pi: e.mul(
                    s_gate[:, oc, sl], s_of[pi][:], s_us[:]),
                   reads=[f"of{pi}", "usB"], writes=[f"g:{oc}:{qc}"])
                op("V", lambda e, oc=oc, sl=sl: e.reduce_max(
                    s_red[:], s_gate[:, oc, sl], AX,
                    apply_absolute_value=True),
                   reads=[f"g:{oc}:{qc}"], writes=["red"])
                op("V", lambda e: e.tensor_max(s_omax[:], s_omax[:], s_red[:]),
                   reads=["red"], writes=["omax"])
        treemax(s_omax, "omax")
        scale_pair(s_omax, "omax", s_os, s_or, "osB", "orB")
        es3c2.close()
        sc.barrier()
        es3p = contextlib.ExitStack()
        sb3p = mk(es3p)
        s_oq = [sb3p(f"s_oq{i}", [128, 512], BF16) for i in range(2)]
        s_x2r = [sb3p(f"s_x2r{i}", [128, 512], F32) for i in range(4)]
        s_x3 = [sb3p(f"s_x3{i}", [128, 512], F32) for i in range(2)]

        banks = [pA[:, 0:512], pA[:, 512:1024], pB[:, 0:512], pB[:, 512:1024],
                 p4[:], p5[:], p6[:], p7[:]]
        bkeys = ["pA", "pA", "pB", "pB", "p4", "p5", "p6", "p7"]
        def x2r_dma(fc, qc):
            op("Y", lambda e, fc=fc, qc=qc: e.dma_start(
                out=s_x2r[fc % 4][:],
                in_=x2sp[fc, :, 512 * qc:512 * (qc + 1)]),
               reads=[f"x2s:{fc}"], writes=[f"x2r{fc % 4}"], dma=True)

        for qc in range(NQC):
            for fc in range(4):
                x2r_dma(fc, qc)
            for ic in range(FC):
                sl = slice(512 * qc, 512 * (qc + 1))
                pi = ic % 2
                op("S", lambda e, ic=ic, sl=sl, pi=pi: e.mul(
                    s_qi[pi][:], s_gate[:, ic, sl], s_or[:]),
                   reads=[f"g:{ic}:{qc}", "orB"], writes=[f"qi{pi}"])
                op("S", lambda e, pi=pi: e.mul(
                    s_oq[pi][:], s_qi[pi][:], s_os[:]),
                   reads=[f"qi{pi}", "osB"], writes=[f"oq{pi}"])
                for gi in range(DC):
                    op("T", lambda e, ic=ic, gi=gi, pi=pi: e.matmul(
                        banks[gi], s_wdb[gi][:, ic, :], s_oq[pi][:],
                        start=(ic == 0), stop=(ic == FC - 1),
                        skip_group_check=True),
                       reads=[f"wdb{gi}", f"oq{pi}"], writes=[bkeys[gi]])
            for fc in range(DC):
                xi2 = fc % 2
                op("V", lambda e, fc=fc, xi2=xi2: e.tensor_add(
                    s_x3[xi2][:], banks[fc], s_x2r[fc % 4][:]),
                   reads=[bkeys[fc], f"x2r{fc % 4}"], writes=[f"x3{xi2}"])
                if fc + 4 < DC:
                    x2r_dma(fc + 4, qc)
                op("Y", lambda e, fc=fc, qc=qc, xi2=xi2: e.dma_start(
                    out=yo[128 * fc:128 * (fc + 1),
                           512 * qc:512 * (qc + 1)], in_=s_x3[xi2][:]),
                   reads=[f"x3{xi2}"], writes=[f"yo:{fc}:{qc}"], dma=True)
        es3p.close()
        es3d.close()
        es3.close()

        planned = sc.plan()
        per = {k: [] for k in "TVSGY"}
        for ek, fn, waits, amt in planned:
            per[ek].append((fn, waits, amt))
        ndma_i = [0]

        @blk.sync
        def _(sync):
            with nc.allow_low_precision(reason="bf16/f16 staging by design"):
                for fn, waits, amt in per["Y"]:
                    for s, v in waits:
                        sync.wait_ge(sems[s], v)
                    slot = f"Y{ndma_i[0] % 16}"
                    ndma_i[0] += 1
                    fn(sync).then_inc(sems[slot], 16)
                for s, v in sc.final_dma.items():
                    if v:
                        sync.wait_ge(sems[s], v)

        @blk.tensor
        def _(tensor):
            with nc.allow_low_precision(reason="bf16/f16 staging by design"):
                for fn, waits, amt in per["T"]:
                    for s, v in waits:
                        tensor.wait_ge(sems[s], v)
                    fn(tensor).then_inc(sems["T"], amt)

        @blk.vector
        def _(vector):
            with nc.allow_low_precision(reason="bf16/f16 staging by design"):
                for fn, waits, amt in per["V"]:
                    for s, v in waits:
                        vector.wait_ge(sems[s], v)
                    fn(vector).then_inc(sems["V"], amt)

        @blk.scalar
        def _(scalar):
            with nc.allow_low_precision(reason="bf16/f16 staging by design"):
                for fn, waits, amt in per["S"]:
                    for s, v in waits:
                        scalar.wait_ge(sems[s], v)
                    fn(scalar).then_inc(sems["S"], amt)

        @blk.gpsimd
        def _(gpsimd):
            with nc.allow_low_precision(reason="bf16/f16 staging by design"):
                for fn, waits, amt in per["G"]:
                    for s, v in waits:
                        gpsimd.wait_ge(sems[s], v)
                    fn(gpsimd).then_inc(sems["G"], amt)

    return nc


_NC_CACHE = {}


def kernel(L0_attn_x, L0_attn_W_q, L0_attn_W_k, L0_attn_W_v, L0_attn_W_o,
           L0_ffn_w_gate, L0_ffn_w_up, L0_ffn_w_down):
    x = np.asarray(L0_attn_x, np.float32)
    perm = head_perm()
    wq_t = np.ascontiguousarray(np.asarray(L0_attn_W_q, np.float32)
                                .T[:, perm].reshape(D, NPAIR, 128)
                                .transpose(1, 0, 2))
    wk_t = np.ascontiguousarray(np.asarray(L0_attn_W_k, np.float32)
                                .T[:, perm].reshape(D, NPAIR, 128)
                                .transpose(1, 0, 2))
    wv_t = np.ascontiguousarray(np.asarray(L0_attn_W_v, np.float32)
                                .T.reshape(D, NPAIR, 128).transpose(1, 0, 2))
    wo_t = np.ascontiguousarray(np.asarray(L0_attn_W_o, np.float32).T)
    wup = np.ascontiguousarray(np.asarray(L0_ffn_w_up, np.float32))
    wg = np.ascontiguousarray(np.asarray(L0_ffn_w_gate, np.float32))
    wd = np.ascontiguousarray(np.asarray(L0_ffn_w_down, np.float32))
    wupt = np.ascontiguousarray(wup.T.reshape(D, FC, 128).transpose(1, 0, 2))
    wgt_a = np.ascontiguousarray(wg.T.reshape(D, FC, 128).transpose(1, 0, 2))
    wdt_a = np.ascontiguousarray(wd.T.reshape(FF, DC, 128).transpose(1, 0, 2))
    p128 = np.arange(128)
    sel8 = np.zeros((32, DC, 128), np.float32)
    for ic in range(DC):
        for bb in range(32):
            sel8[bb, ic, :] = (bb == 4 * ic + p128 // 32)
    sel32 = np.zeros((32, 8, 128), np.float32)
    for j in range(8):
        for bb in range(32):
            sel32[bb, j, :] = (bb == 4 * j + p128 // 32)


    if "nc" not in _NC_CACHE:
        _NC_CACHE["nc"] = build()
    nc = _NC_CACHE["nc"]

    in_maps = []
    for c in range(8):
        b, h = c // 2, c % 2
        roll = h * TOK
        xt = np.ascontiguousarray(np.roll(x[b].T, -roll, axis=1))
        cmap, smap = rope_maps(roll)
        in_maps.append({
            "xi": xt, "wq": wq_t, "wk": wk_t, "wv": wv_t, "wo": wo_t,
            "wupt": wupt, "wgt": wgt_a, "wdt": wdt_a,
            "wup_oi": wup, "wg_oi": wg, "wd_oi": wd,
            "cmap": cmap, "smap": smap, "sel8": sel8, "sel32": sel32,
        })
    res = run_bass_kernel_spmd(nc, in_maps, core_ids=list(range(8)))
    out = np.empty((B, L, D), np.float32)
    for c in range(8):
        b, h = c // 2, c % 2
        yo_ = np.asarray(res.results[c]["yo"])
        out[b, h * TOK:(h + 1) * TOK, :] = yo_.T
    return out
